# version 15
# baseline (speedup 1.0000x reference)
import numpy as np
import concourse.bass as bass
import concourse.mybir as mybir
from concourse.bass_utils import run_bass_kernel_spmd

F32 = mybir.dt.float32
BF16 = mybir.dt.bfloat16
I32 = mybir.dt.int32
AF = mybir.ActivationFunctionType
ALU = mybir.AluOpType
AX = mybir.AxisListType
PI = float(np.pi)
INV2PI = 1.0 / (2.0 * PI)

DEBUG = False
NCORE = 8
SEG = 4096
T = 512
NT = SEG // T
W = 128
NCK = T // W
D = 1024
NST = 256
DH1 = 4096
DH2 = 3072


class _Fake:
    def then_inc(self, *a, **k):
        return self


_FAKE = _Fake()


class EngProxy:
    def __init__(self, real, active):
        object.__setattr__(self, "_real", real)
        object.__setattr__(self, "_active", active)
        object.__setattr__(self, "_last", None)

    def __getattr__(self, name):
        if object.__getattribute__(self, "_active"):
            fn = getattr(object.__getattribute__(self, "_real"), name)
            if not callable(fn):
                return fn

            def wrap(*a, **k):
                r = fn(*a, **k)
                object.__setattr__(self, "_last", r)
                return r
            return wrap
        return lambda *a, **k: _FAKE


class Ctx:
    def __init__(self, nc, role, sems):
        self.nc = nc
        self.role = role
        self.sems = sems
        self.cnt = {k: 0 for k in sems}
        self.pe = EngProxy(nc.tensor, role == "pe")
        self.vec = EngProxy(nc.vector, role == "vec")
        self.scl = EngProxy(nc.scalar, role == "scl")
        self.gps = EngProxy(nc.gpsimd, role == "gps")
        self.snc = EngProxy(nc.sync, role == "snc")

    def mark_last(self, proxy, sem, n=1):
        ins = object.__getattribute__(proxy, "_last") \
            if object.__getattribute__(proxy, "_active") else None
        return self.mark(ins, sem, n)

    def mark(self, ins, sem, n=1):
        self.cnt[sem] += n
        if ins is not None and ins is not _FAKE:
            ins.then_inc(self.sems[sem], n)
        return (sem, self.cnt[sem])

    def dma(self, eng, out, in_, sem):
        ins = eng.dma_start(out=out, in_=in_)
        return self.mark(ins, sem, 16)

    def wait(self, eng, ev):
        if ev is None:
            return
        sem, val = ev
        if val > 0:
            eng.wait_ge(self.sems[sem], val)

    def at(self, sem):
        return (sem, self.cnt[sem])


class Ring:

    def __init__(self, nslots, sembase):
        self.n = nslots
        self.free_ev = [None] * nslots
        self.sems = [f"{sembase}{s}" for s in range(nslots)]
        self.i = 0

    def tick(self):
        s = self.i % self.n
        self.i += 1
        return s


def build_nc():
    nc = bass.Bass("TRN2", target_bir_lowering=False, debug=False)

    x_in = nc.dram_tensor("x", [SEG, D], F32, kind="ExternalInput").ap()
    BreT_i = nc.dram_tensor("BreT", [D, NST], F32, kind="ExternalInput").ap()
    BimT_i = nc.dram_tensor("BimT", [D, NST], F32, kind="ExternalInput").ap()
    CreT_i = nc.dram_tensor("CreT", [NST, D], F32, kind="ExternalInput").ap()
    CimT_i = nc.dram_tensor("CimT", [NST, D], F32, kind="ExternalInput").ap()
    DT_i = nc.dram_tensor("DT", [D, D], F32, kind="ExternalInput").ap()
    W1_i = nc.dram_tensor("W1", [D, DH1], F32, kind="ExternalInput").ap()
    W2_i = nc.dram_tensor("W2", [DH1, D], F32, kind="ExternalInput").ap()
    WGT_i = nc.dram_tensor("WGT", [D, DH2], I32, kind="ExternalInput").ap()
    WUT_i = nc.dram_tensor("WUT", [D, DH2], I32, kind="ExternalInput").ap()
    WDT_i = nc.dram_tensor("WDT", [DH2, D], I32, kind="ExternalInput").ap()
    nu_log = nc.dram_tensor("nu_log", [NST, 1], F32, kind="ExternalInput").ap()
    th_log = nc.dram_tensor("th_log", [NST, 1], F32, kind="ExternalInput").ap()
    rsl = nc.dram_tensor("rsl", [128, 1], F32, kind="ExternalInput").ap()
    msel_i = nc.dram_tensor("msel", [128, 24], F32, kind="ExternalInput").ap()
    out_o = nc.dram_tensor("out", [SEG, D], F32, kind="ExternalOutput").ap()

    _skind = "ExternalOutput" if DEBUG else "Internal"
    sW1 = nc.dram_tensor("sW1", [D, DH1], BF16, kind=_skind).ap()
    sGU = nc.dram_tensor("sGU", [D, 8, 768], BF16, kind=_skind).ap()
    sW2 = nc.dram_tensor("sW2", [8, 128, DH1], BF16, kind=_skind).ap()
    sWD = nc.dram_tensor("sWD", [8, 128, DH2], BF16, kind=_skind).ap()
    sBC = nc.dram_tensor("sBC", [D, 512], BF16, kind=_skind).ap()
    sDT = nc.dram_tensor("sDT", [D, D], BF16, kind=_skind).ap()
    sY = nc.dram_tensor("sY", [NT, 128, 8 * T], BF16, kind=_skind).ap()
    sXN = nc.dram_tensor("sXN", [NT, 128, 8 * T], BF16, kind=_skind).ap()
    sG = nc.dram_tensor("sG", [NT, 128, 8 * T], BF16, kind=_skind).ap()
    cc_in = nc.dram_tensor("cc_in", [512], F32).ap()
    cc_out = nc.dram_tensor("cc_out", [NCORE * 512], F32).ap()

    from contextlib import ExitStack
    es = ExitStack()
    with es:
        def sbt(name, shape, dt=F32):
            return es.enter_context(nc.sbuf_tensor(name, shape, dt))

        sb_CreT = sbt("sb_CreT", [128, 2 * D], BF16)
        sb_CimnT = sbt("sb_CimnT", [128, 2 * D], BF16)
        sb_iota = sbt("sb_iota", [128, T])
        sb_iot1 = sbt("sb_iot1", [128, W])
        sb_cosA = sbt("sb_cosA", [128, 2 * T])
        sb_sinA = sbt("sb_sinA", [128, 2 * T])
        sb_rpw = sbt("sb_rpw", [128, 2 * T])
        sb_pw = sbt("sb_pw", [128, 2 * W])
        sb_rb = sbt("sb_rb", [128, 2 * W])
        sb_rho = sbt("sb_rho", [128, 2 * NCK])
        sb_theta = sbt("sb_theta", [128, 2])
        sb_lnr = sbt("sb_lnr", [128, 2])
        sb_r = sbt("sb_r", [128, 2])
        sb_gam = sbt("sb_gam", [128, 2])
        sb_cs1 = sbt("sb_cs1", [128, 4])
        sb_rot = sbt("sb_rot", [128, 4])
        sb_P512 = sbt("sb_P512", [128, 4])
        sb_S0 = sbt("sb_S0", [128, 8])
        sb_sc4 = sbt("sb_sc4", [128, 8])
        sb_sig = sbt("sb_sig", [128, 1])
        sb_identb = sbt("sb_identb", [128, 128], BF16)
        sb_mstat = sbt("sb_mstat", [128, 32])
        sb_nstat = sbt("sb_nstat", [128, 32])
        sb_msel = sbt("sb_msel", [128, 24])
        sb_Hin = sbt("sb_Hin", [128, 4])
        sb_H = sbt("sb_H", [128, 4 * NCK])
        sb_S = sbt("sb_S", [128, 4])
        sb_gath = sbt("sb_gath", [128, 32])
        sb_TK = sbt("sb_TK", [128, 12])
        sb_P = sbt("sb_P", [128, 12])
        sb_acc = sbt("sb_acc", [128, 4])
        sb_acc2 = sbt("sb_acc2", [128, 4])
        sb_x = sbt("sb_x", [128, 4 * D])
        sb_xn = sbt("sb_xn", [128, 4 * D], BF16)
        sb_xnf = sbt("sb_xnf", [128, 8 * T], BF16)
        sb_gre = sbt("sb_gre", [128, T])
        sb_gim = sbt("sb_gim", [128, T])
        sb_bu = sbt("sb_bu", [128, 2 * T])
        sb_scr = sbt("sb_scr", [128, 2 * T])
        sb_mi = sbt("sb_mi", [128, T], I32)
        sb_tA = sbt("sb_tA", [128, T])
        sb_tB = sbt("sb_tB", [128, T])
        sb_hre = sbt("sb_hre", [128, 2 * T], BF16)
        sb_himn = sbt("sb_himn", [128, 2 * T], BF16)
        sb_y = sbt("sb_y", [128, 8 * T], BF16)
        sb_gA = sbt("sb_gA", [128, 8 * T], BF16)
        sb_big = sbt("sb_big", [128, 32 * T], BF16)
        sb_br = sbt("sb_br", [128, 2 * 512], BF16)
        sb_dr = sbt("sb_dr", [128, 2 * D], BF16)
        sb_gur = sbt("sb_gur", [128, 4 * 768], BF16)
        sb_wdr = sbt("sb_wdr", [128, 2 * DH2], BF16)
        sb_w1r = sbt("sb_w1r", [128, 4 * 512], BF16)
        sb_w2r = sbt("sb_w2r", [128, 2 * DH1], BF16)
        sb_stg = sbt("sb_stg", [128, 2 * 1024])
        sb_stgi = sbt("sb_stgi", [128, 2 * 1024], I32)
        sb_stgo = sbt("sb_stgo", [128, 2 * 1024], BF16)

        pb = [es.enter_context(nc.psum_tensor(f"pb{i}", [128, 512], F32))
              for i in range(6)]
        ptx = [es.enter_context(nc.psum_tensor(f"ptx{i}", [128, 512], BF16))
               for i in range(2)]
        pto = ptx

        sems = {}
        for name in (["xq", "pq", "rly", "rlx", "cg", "gq", "sv", "ss", "sg", "sp", "cc",
                      "st0", "st1", "si0", "si1", "so0", "so1",
                      "br0", "br1", "dr0", "dr1",
                      "gu0", "gu1", "gu2", "gu3", "wd0", "wd1",
                      "w10", "w11", "w12", "w13", "w20", "w21"]):
            sems[name] = es.enter_context(nc.semaphore(name))

        block = es.enter_context(nc.Block())

        def emit_program(C: Ctx):
            V, S_, G, P_, Q = C.vec, C.scl, C.gps, C.pe, C.snc

            def VF():
                C.wait(V, C.mark_last(V, "sv"))

            bank_ev = [None] * 6
            ptx_ev = [None] * 2
            pto_ev = ptx_ev

            C.dma(Q, sb_theta[:, 0:1], th_log[0:128, :], "pq")
            C.dma(Q, sb_theta[:, 1:2], th_log[128:256, :], "pq")
            C.dma(Q, sb_lnr[:, 0:1], nu_log[0:128, :], "pq")
            C.dma(Q, sb_lnr[:, 1:2], nu_log[128:256, :], "pq")
            C.dma(Q, sb_sig[:], rsl, "pq")
            ev_params = C.dma(Q, sb_msel[:], msel_i, "pq")

            ev_x_free = [None]

            def load_x(tile):
                C.wait(Q, ev_x_free[0])
                ev = None
                for tc in range(4):
                    ev = C.dma(Q, sb_x[:, tc * D:(tc + 1) * D],
                               x_in[(tile * 4 + tc) * 128:(tile * 4 + tc + 1) * 128, :],
                               "xq")
                return ev

            ev_x0 = load_x(0)

            G.iota(sb_iota[:], pattern=[[1, T]], base=0, channel_multiplier=0,
                   allow_small_or_imprecise_dtypes=True)
            G.iota(sb_iot1[:], pattern=[[1, W]], base=1, channel_multiplier=0,
                   allow_small_or_imprecise_dtypes=True)
            G.memset(sb_identb[:], 0.0)
            G.affine_select(out=sb_identb[:], in_=sb_identb[:],
                            compare_op=ALU.not_equal, fill=1.0, base=0,
                            pattern=[[-1, 128]], channel_multiplier=1)
            G.memset(sb_Hin[:], 0.0)
            ev_gsetup = C.mark_last(G, "sg")

            C.wait(S_, ev_params)
            S_.activation(out=sb_theta[:, 0:2], in_=sb_theta[:, 0:2], func=AF.Exp)
            S_.activation(out=sb_r[:, 0:2], in_=sb_lnr[:, 0:2], func=AF.Exp)
            S_.activation(out=sb_sig[:], in_=sb_sig[:], func=AF.Sigmoid)
            ev = C.mark_last(S_, "ss")
            C.wait(V, ev)
            V.tensor_scalar_mul(sb_lnr[:, 0:2], sb_r[:, 0:2], -1.0)
            ev = C.mark_last(V, "sv")
            C.wait(S_, ev)
            S_.activation(out=sb_r[:, 0:2], in_=sb_lnr[:, 0:2], func=AF.Exp)
            C.wait(S_, ev_gsetup)
            for nch in range(2):
                S_.activation(out=sb_pw[:, nch * W:(nch + 1) * W], in_=sb_iot1[:],
                              func=AF.Exp, scale=sb_lnr[:, nch:nch + 1])
            ev_s2 = C.mark_last(S_, "ss")
            C.wait(V, ev_s2)
            for nch in range(2):
                V.tensor_scalar(out=sb_scr[:, nch * T:(nch + 1) * T], in0=sb_iota[:],
                                scalar1=sb_lnr[:, nch:nch + 1], scalar2=-85.0,
                                op0=ALU.mult, op1=ALU.max)
            ev = C.mark_last(V, "sv")
            C.wait(S_, ev)
            for nch in range(2):
                S_.activation(out=sb_rpw[:, nch * T:(nch + 1) * T],
                              in_=sb_scr[:, nch * T:(nch + 1) * T], func=AF.Exp)
            ev_rpw = C.mark_last(S_, "ss")
            C.wait(V, ev_rpw)
            V.tensor_mul(sb_gam[:, 0:2], sb_r[:, 0:2], sb_r[:, 0:2])
            VF()
            V.tensor_scalar(out=sb_gam[:, 0:2], in0=sb_gam[:, 0:2], scalar1=-1.0,
                            scalar2=1.0, op0=ALU.mult, op1=ALU.add)
            ev = C.mark_last(V, "sv")
            C.wait(S_, ev)
            S_.activation(out=sb_gam[:, 0:2], in_=sb_gam[:, 0:2], func=AF.Sqrt)
            ev_gam = C.mark_last(S_, "ss")
            C.wait(V, ev_gam)
            V.memset(sb_rb[:], 1.0)
            V.memset(sb_rho[:], 1.0)
            V.memset(sb_sc4[:, 4:5], PI / 2.0)
            for nch in range(2):
                V.tensor_scalar_mul(sb_rb[:, nch * W:(nch + 1) * W],
                                    sb_rb[:, nch * W:(nch + 1) * W],
                                    sb_r[:, nch:nch + 1])
                V.tensor_scalar_mul(sb_rho[:, nch * NCK:(nch + 1) * NCK],
                                    sb_rho[:, nch * NCK:(nch + 1) * NCK],
                                    sb_pw[:, nch * W + W - 1:nch * W + W])
            ev_bcast = C.mark_last(V, "sv")

            def trig_cols(val, dst):
                V.tensor_scalar_mul(sb_sc4[:, 0:2], sb_theta[:, 0:2], val * INV2PI)
                VF()
                V.tensor_copy(out=sb_mi[:, 0:2], in_=sb_sc4[:, 0:2])
                VF()
                V.tensor_copy(out=sb_sc4[:, 2:4], in_=sb_mi[:, 0:2])
                VF()
                V.tensor_sub(sb_sc4[:, 0:2], sb_sc4[:, 0:2], sb_sc4[:, 2:4])
                VF()
                V.tensor_scalar_mul(sb_sc4[:, 0:2], sb_sc4[:, 0:2], PI / 2.0)
                ev = C.mark_last(V, "sv")
                C.wait(S_, ev)
                S_.activation(out=sb_sc4[:, 2:4], in_=sb_sc4[:, 0:2], func=AF.Sin)
                S_.activation(out=sb_sc4[:, 6:8], in_=sb_sc4[:, 0:2], func=AF.Sin,
                              scale=-1.0, bias=sb_sc4[:, 4:5])
                ev = C.mark_last(S_, "ss")
                C.wait(V, ev)
                V.tensor_mul(sb_sc4[:, 0:2], sb_sc4[:, 2:4], sb_sc4[:, 6:8])
                VF()
                V.tensor_scalar_mul(sb_sc4[:, 0:2], sb_sc4[:, 0:2], 2.0)
                V.tensor_mul(dst[:, 0:2], sb_sc4[:, 2:4], sb_sc4[:, 2:4])
                VF()
                V.tensor_scalar(out=dst[:, 0:2], in0=dst[:, 0:2], scalar1=-2.0,
                                scalar2=1.0, op0=ALU.mult, op1=ALU.add)
                VF()
                V.tensor_mul(dst[:, 2:4], sb_sc4[:, 0:2], dst[:, 0:2])
                VF()
                V.tensor_scalar_mul(dst[:, 2:4], dst[:, 2:4], 2.0)
                V.tensor_mul(sb_sc4[:, 2:4], sb_sc4[:, 0:2], sb_sc4[:, 0:2])
                VF()
                V.tensor_scalar(out=dst[:, 0:2], in0=sb_sc4[:, 2:4], scalar1=-2.0,
                                scalar2=1.0, op0=ALU.mult, op1=ALU.add)
                return C.mark_last(V, "sv")

            trig_cols(1.0, sb_cs1)
            ev_rot = trig_cols(512.0, sb_rot)

            ev_tables = [None]

            def build_tables_tile0():
                for nch in range(2):
                    cs = sb_cosA[:, nch * T:(nch + 1) * T]
                    sn = sb_sinA[:, nch * T:(nch + 1) * T]
                    V.tensor_scalar(out=sb_tA[:], in0=sb_iota[:],
                                    scalar1=sb_theta[:, nch:nch + 1], scalar2=INV2PI,
                                    op0=ALU.mult, op1=ALU.mult)
                    V.tensor_copy(out=sb_mi[:], in_=sb_tA[:])
                    V.tensor_copy(out=sb_tB[:], in_=sb_mi[:])
                    V.tensor_sub(sb_tA[:], sb_tA[:], sb_tB[:])
                    V.tensor_scalar_mul(sb_tA[:], sb_tA[:], PI / 2.0)
                    ev = C.mark_last(V, "sv")
                    C.wait(S_, ev)
                    S_.activation(out=sb_tB[:], in_=sb_tA[:], func=AF.Sin)
                    S_.activation(out=sb_gre[:], in_=sb_tA[:], func=AF.Sin,
                                  scale=-1.0, bias=sb_sc4[:, 4:5])
                    ev = C.mark_last(S_, "ss")
                    C.wait(V, ev)
                    V.tensor_mul(sb_tA[:], sb_tB[:], sb_gre[:])
                    V.tensor_scalar_mul(sb_tA[:], sb_tA[:], 2.0)
                    V.tensor_mul(cs, sb_tB[:], sb_tB[:])
                    V.tensor_scalar(out=cs, in0=cs, scalar1=-2.0, scalar2=1.0,
                                    op0=ALU.mult, op1=ALU.add)
                    V.tensor_mul(sn, sb_tA[:], cs)
                    V.tensor_scalar_mul(sn, sn, 2.0)
                    V.tensor_mul(sb_tB[:], sb_tA[:], sb_tA[:])
                    V.tensor_scalar(out=cs, in0=sb_tB[:], scalar1=-2.0, scalar2=1.0,
                                    op0=ALU.mult, op1=ALU.add)
                ev_tables[0] = C.mark_last(V, "sv")

            def rotate_tables():
                C.wait(V, ev_rot)
                for nch in range(2):
                    oc = sb_cosA[:, nch * T:(nch + 1) * T]
                    os_ = sb_sinA[:, nch * T:(nch + 1) * T]
                    c5 = sb_rot[:, nch:nch + 1]
                    s5 = sb_rot[:, 2 + nch:3 + nch]
                    V.tensor_scalar_mul(sb_tA[:], oc, c5)
                    V.tensor_scalar_mul(sb_tB[:], os_, s5)
                    V.tensor_sub(sb_gre[:], sb_tA[:], sb_tB[:])
                    V.tensor_scalar_mul(sb_tA[:], os_, c5)
                    V.tensor_scalar_mul(sb_tB[:], oc, s5)
                    V.tensor_add(sb_gim[:], sb_tA[:], sb_tB[:])
                    V.tensor_copy(out=oc, in_=sb_gre[:])
                    V.tensor_copy(out=os_, in_=sb_gim[:])
                ev_tables[0] = C.mark_last(V, "sv")

            cast_state = {
                0: {"stg": None, "sti": None, "sto": None},
                1: {"stg": None, "sti": None, "sto": None},
            }
            cast_ix = [0]

            def cast_job(src, dst, int_src=False, dst_sb=None, dst_view=None):
                h = cast_ix[0] % 2
                cast_ix[0] += 1
                E = S_ if h == 0 else V
                esem = "ss" if h == 0 else "sv"
                st = cast_state[h]
                ncols = src.shape[1]
                if int_src:
                    stgb = sb_stgi[:, h * 1024:h * 1024 + ncols]
                    lsem = f"si{h}"
                    C.wait(Q, st["sti"])
                else:
                    stgb = sb_stg[:, h * 1024:h * 1024 + ncols]
                    lsem = f"st{h}"
                    C.wait(Q, st["stg"])
                evl = C.dma(Q, stgb, src, lsem)
                C.wait(E, evl)
                if dst_sb is not None:
                    ins = E.tensor_copy(out=dst_sb, in_=stgb) if h == 1 else \
                        E.copy(out=dst_sb, in_=stgb)
                    evc = C.mark(ins, esem)
                    if int_src:
                        st["sti"] = evc
                    else:
                        st["stg"] = evc
                    return evc
                C.wait(E, st["sto"])
                ob = sb_stgo[:, h * 1024:h * 1024 + ncols]
                ins = E.tensor_copy(out=ob, in_=stgb) if h == 1 else \
                    E.copy(out=ob, in_=stgb)
                evc = C.mark(ins, esem)
                if int_src:
                    st["sti"] = evc
                else:
                    st["stg"] = evc
                C.wait(G, evc)
                if dst_view is not None:
                    evs = C.dma(G, dst, dst_view(ob), f"so{h}")
                else:
                    evs = C.dma(G, dst, ob, f"so{h}")
                st["sto"] = evs
                return evs

            ev_Cres = None
            for nch in range(2):
                cast_job(CreT_i[nch * 128:(nch + 1) * 128, :],
                         None, dst_sb=sb_CreT[:, nch * D:(nch + 1) * D])
                ev_Cres = cast_job(CimT_i[nch * 128:(nch + 1) * 128, :],
                                   None, dst_sb=sb_CimnT[:, nch * D:(nch + 1) * D])
            ev_Cres2 = C.at("ss")
            ev_Cres3 = C.at("sv")

            for dc in range(8):
                cast_job(BreT_i[dc * 128:(dc + 1) * 128, :],
                         sBC[dc * 128:(dc + 1) * 128, 0:256])
                cast_job(BimT_i[dc * 128:(dc + 1) * 128, :],
                         sBC[dc * 128:(dc + 1) * 128, 256:512])
            ev_B = (C.at("so0"), C.at("so1"))

            cast_GU = {}

            def emit_gu_casts(jg):
                for k in range(8):
                    for which, srcm in (("g", WGT_i), ("u", WUT_i)):
                        hoff = 0 if which == "g" else 384
                        src = srcm[k * 128:(k + 1) * 128, jg * 768:(jg + 1) * 768]
                        dst = sGU[k * 128:(k + 1) * 128, 2 * jg:2 * jg + 2, hoff:hoff + 384]
                        ev = cast_job(src, dst, int_src=True,
                                      dst_view=lambda ob: ob.rearrange(
                                          "p (g c) -> p g c", c=384))
                        cast_GU[(k, jg, which)] = ev

            def emit_wd_casts():
                for n in range(24):
                    cast_job(WDT_i[n * 128:(n + 1) * 128, :],
                             sWD[:, :, n * 128:(n + 1) * 128].rearrange(
                                 "m p c -> p m c"),
                             int_src=True,
                             dst_view=lambda ob: ob.rearrange(
                                 "p (m c) -> p m c", m=8))
                return (C.at("so0"), C.at("so1"))

            def emit_d_casts():
                for dc in range(8):
                    cast_job(DT_i[dc * 128:(dc + 1) * 128, :],
                             sDT[dc * 128:(dc + 1) * 128, :])
                return (C.at("so0"), C.at("so1"))

            w12_jobs = []
            for rc in range(8):
                for cp in range(4):
                    w12_jobs.append((W1_i[rc * 128:(rc + 1) * 128,
                                          cp * 1024:(cp + 1) * 1024],
                                     sW1[rc * 128:(rc + 1) * 128,
                                         cp * 1024:(cp + 1) * 1024], False))
            for n in range(32):
                w12_jobs.append((W2_i[n * 128:(n + 1) * 128, :],
                                 sW2[:, :, n * 128:(n + 1) * 128].rearrange(
                                     "m p c -> p m c"), True))
            w12_li = [0]
            w12_ci = [0]
            w12_ld = {}
            w12_stg = {0: None, 1: None}
            w12_sto = {0: None, 1: None}

            def w12_load(n):
                for _ in range(n):
                    j = w12_li[0]
                    if j >= len(w12_jobs):
                        return
                    w12_li[0] = j + 1
                    h = j % 2
                    if w12_stg[h] is None:
                        w12_stg[h] = cast_state[h]["stg"]
                    C.wait(Q, w12_stg[h])
                    src = w12_jobs[j][0]
                    w12_ld[j] = C.dma(Q, sb_stg[:, h * 1024:(h + 1) * 1024],
                                      src, f"st{h}")

            def w12_copy_store(n):
                for _ in range(n):
                    j = w12_ci[0]
                    if j >= min(w12_li[0], len(w12_jobs)):
                        return
                    w12_ci[0] = j + 1
                    h = j % 2
                    if w12_sto[h] is None:
                        w12_sto[h] = cast_state[h]["sto"]
                    C.wait(S_, w12_ld[j])
                    C.wait(S_, w12_sto[h])
                    ins = S_.copy(out=sb_stgo[:, h * 1024:(h + 1) * 1024],
                                  in_=sb_stg[:, h * 1024:(h + 1) * 1024])
                    evc = C.mark(ins, "ss")
                    w12_stg[h] = evc
                    C.wait(Q, evc)
                    src, dst, rearr = w12_jobs[j]
                    ob = sb_stgo[:, h * 1024:(h + 1) * 1024]
                    if rearr:
                        ob = ob.rearrange("p (m c) -> p m c", m=8)
                    w12_sto[h] = C.dma(Q, dst, ob, f"so{h}")

            ev_xn_tile = [None]

            def ln_tile(tile, ev_xload):
                C.wait(V, ev_xload)
                b4 = slice(tile * 4, tile * 4 + 4)
                for tc in range(4):
                    xa = sb_x[:, tc * D:(tc + 1) * D]
                    sc = slice(tile * 4 + tc, tile * 4 + tc + 1)
                    V.tensor_reduce(sb_mstat[:, sc], xa, axis=AX.X, op=ALU.add)
                VF()
                V.tensor_scalar_mul(sb_mstat[:, b4], sb_mstat[:, b4], 1.0 / D)
                VF()
                for tc in range(4):
                    xa = sb_x[:, tc * D:(tc + 1) * D]
                    sc = slice(tile * 4 + tc, tile * 4 + tc + 1)
                    V.scalar_tensor_tensor(out=sb_scr[:, 0:D], in0=xa,
                                           scalar=sb_mstat[:, sc], in1=xa,
                                           op0=ALU.subtract, op1=ALU.mult,
                                           accum_out=sb_nstat[:, sc])
                VF()
                V.tensor_scalar(out=sb_nstat[:, b4], in0=sb_nstat[:, b4],
                                scalar1=1.0 / D, scalar2=1e-5,
                                op0=ALU.mult, op1=ALU.add)
                ev = C.mark_last(V, "sv")
                C.wait(S_, ev)
                S_.activation(out=sb_nstat[:, b4], in_=sb_nstat[:, b4],
                              func=AF.Sqrt)
                ev = C.mark_last(S_, "ss")
                C.wait(V, ev)
                V.reciprocal(out=sb_nstat[:, b4], in_=sb_nstat[:, b4])
                VF()
                for tc in range(4):
                    xa = sb_x[:, tc * D:(tc + 1) * D]
                    sc = slice(tile * 4 + tc, tile * 4 + tc + 1)
                    V.tensor_scalar(out=sb_xn[:, tc * D:(tc + 1) * D], in0=xa,
                                    scalar1=sb_mstat[:, sc], scalar2=sb_nstat[:, sc],
                                    op0=ALU.subtract, op1=ALU.mult)
                ev_xn_tile[0] = C.mark_last(V, "sv")

            ev_xnf = [None]
            ev_xnf_free = [None, None]

            def transpose_xn():
                C.wait(P_, ev_xn_tile[0])
                for dc in range(8):
                    b = dc % 2
                    C.wait(P_, ptx_ev[b])
                    ins = None
                    for tc in range(4):
                        ins = P_.transpose(ptx[b][:, tc * 128:(tc + 1) * 128],
                                           sb_xn[:, tc * D + dc * 128:tc * D + (dc + 1) * 128],
                                           sb_identb[:])
                    ev = C.mark(ins, "sp")
                    C.wait(S_, ev)
                    if dc < 2:
                        C.wait(S_, ev_xnf_free[0])
                        C.wait(S_, ev_xnf_free[1])
                    ins = S_.copy(out=sb_xnf[:, dc * T:(dc + 1) * T], in_=ptx[b][:])
                    ptx_ev[b] = C.mark(ins, "ss")
                ev_xnf[0] = ptx_ev[1]

            br_ring = Ring(2, "br")

            def bu_mms(tile):
                for b in range(4):
                    C.wait(P_, bank_ev[b])
                ins = None
                for dc in range(8):
                    slot = br_ring.tick()
                    C.wait(S_, br_ring.free_ev[slot])
                    if tile == 0 and dc < 2:
                        C.wait(S_, ev_B[0])
                        C.wait(S_, ev_B[1])
                    eld = C.dma(S_, sb_br[:, slot * 512:(slot + 1) * 512],
                                sBC[dc * 128:(dc + 1) * 128, :],
                                br_ring.sems[slot])
                    C.wait(P_, eld)
                    if dc == 0:
                        C.wait(P_, ev_xnf[0])
                    rhs = sb_xnf[:, dc * T:(dc + 1) * T]
                    for nch in range(2):
                        P_.matmul(pb[nch][:],
                                  sb_br[:, slot * 512 + nch * 128:slot * 512 + (nch + 1) * 128],
                                  rhs, start=(dc == 0), stop=(dc == 7))
                        ins = P_.matmul(
                            pb[2 + nch][:],
                            sb_br[:, slot * 512 + 256 + nch * 128:slot * 512 + 256 + (nch + 1) * 128],
                            rhs, start=(dc == 0), stop=(dc == 7))
                    br_ring.free_ev[slot] = C.mark(ins, "sp")
                return br_ring.free_ev[(br_ring.i - 1) % br_ring.n]

            ev_h_cons = [None]

            def tile_scan_stream(tile, ev_bu):
                C.wait(V, ev_bu)
                C.wait(V, ev_tables[0])
                C.wait(V, ev_h_cons[0])
                V.tensor_copy(out=sb_bu[:, 0:T], in_=pb[1][:])
                ins = V.tensor_copy(out=sb_bu[:, T:2 * T], in_=pb[3][:])
                ev_bu1 = C.mark(ins, "sv")
                bank_ev[1] = ev_bu1
                bank_ev[3] = ev_bu1
                for nch in range(2):
                    pre_b = pb[nch][:] if nch == 0 else sb_bu[:, 0:T]
                    pim_b = pb[2 + nch][:] if nch == 0 else sb_bu[:, T:2 * T]
                    cs = sb_cosA[:, nch * T:(nch + 1) * T]
                    sn = sb_sinA[:, nch * T:(nch + 1) * T]
                    gm = sb_gam[:, nch:nch + 1]
                    V.tensor_tensor(out=sb_tA[:], in0=cs, in1=pre_b, op=ALU.mult)
                    V.tensor_tensor(out=sb_tB[:], in0=sn, in1=pim_b, op=ALU.mult)
                    V.tensor_add(sb_tA[:], sb_tA[:], sb_tB[:])
                    V.tensor_scalar_mul(sb_gre[:], sb_tA[:], gm)
                    V.tensor_tensor(out=sb_tA[:], in0=cs, in1=pim_b, op=ALU.mult)
                    V.tensor_tensor(out=sb_tB[:], in0=sn, in1=pre_b, op=ALU.mult)
                    V.tensor_sub(sb_tA[:], sb_tA[:], sb_tB[:])
                    ins = V.tensor_scalar_mul(sb_gim[:], sb_tA[:], gm)
                    ev_mod = C.mark(ins, "sv")
                    if nch == 0:
                        bank_ev[0] = ev_mod
                        bank_ev[2] = ev_mod
                    ins = None
                    for comp in range(2):
                        gbuf = sb_gre if comp == 0 else sb_gim
                        ubuf = sb_scr[:, comp * T:(comp + 1) * T]
                        for c in range(NCK):
                            ins = V.tensor_tensor_scan(
                                out=ubuf[:, c * W:(c + 1) * W],
                                data0=sb_rb[:, nch * W:(nch + 1) * W],
                                data1=gbuf[:, c * W:(c + 1) * W],
                                initial=0.0, op0=ALU.mult, op1=ALU.add)
                    ev = C.mark(ins, "sv")
                    C.wait(V, ev)
                    for comp in range(2):
                        ubuf = sb_scr[:, comp * T:(comp + 1) * T]
                        hcol = sb_H[:, (nch * 2 + comp) * NCK:(nch * 2 + comp + 1) * NCK]
                        ins = V.tensor_tensor_scan(
                            out=hcol,
                            data0=sb_rho[:, nch * NCK:(nch + 1) * NCK],
                            data1=ubuf[:, W - 1:T:W],
                            initial=sb_Hin[:, nch * 2 + comp:nch * 2 + comp + 1],
                            op0=ALU.mult, op1=ALU.add)
                    ev = C.mark(ins, "sv")
                    C.wait(V, ev)
                    for comp in range(2):
                        ubuf = sb_scr[:, comp * T:(comp + 1) * T]
                        hcol = sb_H[:, (nch * 2 + comp) * NCK:(nch * 2 + comp + 1) * NCK]
                        for c in range(NCK):
                            scal = (sb_Hin[:, nch * 2 + comp:nch * 2 + comp + 1]
                                    if c == 0 else hcol[:, c - 1:c])
                            V.scalar_tensor_tensor(
                                out=ubuf[:, c * W:(c + 1) * W],
                                in0=sb_pw[:, nch * W:(nch + 1) * W],
                                scalar=scal,
                                in1=ubuf[:, c * W:(c + 1) * W],
                                op0=ALU.mult, op1=ALU.add)
                    ure = sb_scr[:, 0:T]
                    uim = sb_scr[:, T:2 * T]
                    V.tensor_tensor(out=sb_tA[:], in0=cs, in1=ure, op=ALU.mult)
                    V.tensor_tensor(out=sb_tB[:], in0=sn, in1=uim, op=ALU.mult)
                    V.tensor_sub(sb_hre[:, nch * T:(nch + 1) * T], sb_tA[:], sb_tB[:])
                    V.tensor_tensor(out=sb_tA[:], in0=sn, in1=ure, op=ALU.mult)
                    V.tensor_tensor(out=sb_tB[:], in0=cs, in1=uim, op=ALU.mult)
                    V.tensor_add(sb_tA[:], sb_tA[:], sb_tB[:])
                    V.tensor_scalar_mul(sb_himn[:, nch * T:(nch + 1) * T],
                                        sb_tA[:], -1.0)
                    ins = None
                    for comp in range(2):
                        hcol = sb_H[:, (nch * 2 + comp) * NCK:(nch * 2 + comp + 1) * NCK]
                        ins = V.tensor_copy(
                            out=sb_Hin[:, nch * 2 + comp:nch * 2 + comp + 1],
                            in_=hcol[:, NCK - 1:NCK])
                    ev = C.mark(ins, "sv")
                    C.wait(V, ev)
                return C.at("sv")

            gu_ring = Ring(4, "gu")
            wd_ring = Ring(2, "wd")
            dr_ring = Ring(2, "dr")
            w1_ring = Ring(4, "w1")
            w2_ring = Ring(2, "w2")

            ev_hf = [None] * 24
            ev_g = [None] * 8
            ev_gA_free = [None]
            ev_y_free = [None]
            ev_WD = [None]
            ev_D = [None]

            def gu_stage(tile):
                for ng in range(8):
                    if tile == 0:
                        emit_gu_casts(ng // 2) if ng % 2 == 0 else None
                    insg = insu = None
                    for kk in range(0, 8, 2):
                        slots = []
                        for k in (kk, kk + 1):
                            slot = gu_ring.tick()
                            C.wait(G, gu_ring.free_ev[slot])
                            if tile == 0:
                                C.wait(G, cast_GU[(k, ng // 2, "g")])
                                C.wait(G, cast_GU[(k, ng // 2, "u")])
                            slots.append((k, slot,
                                          C.dma(G, sb_gur[:, slot * 768:(slot + 1) * 768],
                                                sGU[k * 128:(k + 1) * 128, ng, :],
                                                gu_ring.sems[slot])))
                        for k, slot, eld in slots:
                            C.wait(P_, eld)
                        if kk == 0:
                            for nb in range(6):
                                C.wait(P_, bank_ev[nb])
                        for k, slot, eld in slots:
                            for nb in range(3):
                                insg = P_.matmul(
                                    pb[nb][:],
                                    sb_gur[:, slot * 768 + nb * 128:slot * 768 + (nb + 1) * 128],
                                    sb_xnf[:, k * T:(k + 1) * T],
                                    start=(k == 0), stop=(k == 7))
                                insu = P_.matmul(
                                    pb[3 + nb][:],
                                    sb_gur[:, slot * 768 + 384 + nb * 128:slot * 768 + 384 + (nb + 1) * 128],
                                    sb_xnf[:, k * T:(k + 1) * T],
                                    start=(k == 0), stop=(k == 7))
                            gu_ring.free_ev[slot] = C.mark(insu, "sp")
                    evk = gu_ring.free_ev[(gu_ring.i - 1) % gu_ring.n]
                    for nb in range(3):
                        n = ng * 3 + nb
                        C.wait(S_, evk)
                        ins = S_.activation(out=sb_big[:, n * T:(n + 1) * T],
                                            in_=pb[nb][:], func=AF.Silu,
                                            scale=1.0 / 127.0)
                        evsl = C.mark(ins, "ss")
                        bank_ev[nb] = evsl
                        C.wait(V, evsl)
                        ins = V.tensor_tensor(out=sb_big[:, n * T:(n + 1) * T],
                                              in0=sb_big[:, n * T:(n + 1) * T],
                                              in1=pb[3 + nb][:], op=ALU.mult)
                        ev_hf[n] = C.mark(ins, "sv")
                        bank_ev[3 + nb] = ev_hf[n]
                    if tile >= 1:
                        w12_copy_store(1)
                        w12_load(1)

            def wd_stage(tile):
                for m in range(8):
                    b = m % 2
                    C.wait(P_, bank_ev[b])
                    slot = wd_ring.tick()
                    C.wait(G, wd_ring.free_ev[slot])
                    if tile == 0 and m < 2:
                        C.wait(G, ev_WD[0][0])
                        C.wait(G, ev_WD[0][1])
                    eld = C.dma(G, sb_wdr[:, slot * DH2:(slot + 1) * DH2],
                                sWD[m, :, :], wd_ring.sems[slot])
                    C.wait(P_, eld)
                    ins = None
                    for n in range(24):
                        if n == 0:
                            C.wait(P_, ev_hf[23])
                        ins = P_.matmul(
                            pb[b][:],
                            sb_wdr[:, slot * DH2 + n * 128:slot * DH2 + (n + 1) * 128],
                            sb_big[:, n * T:(n + 1) * T],
                            start=(n == 0), stop=(n == 23))
                    wd_ring.free_ev[slot] = C.mark(ins, "sp")
                    C.wait(S_, wd_ring.free_ev[slot])
                    if m == 0:
                        C.wait(S_, ev_gA_free[0])
                    ins = S_.activation(out=sb_gA[:, m * T:(m + 1) * T],
                                        in_=pb[b][:], func=AF.Silu,
                                        scale=1.0 / (127.0 * 127.0))
                    ev_g[m] = C.mark(ins, "ss")
                    bank_ev[b] = ev_g[m]

            ev_y = [None] * 8

            def y_stage(tile, ev_h):
                for half in range(2):
                    ms = range(4 * half, 4 * half + 4)
                    for m in ms:
                        b = 2 + (m % 4)
                        C.wait(P_, bank_ev[b])
                        if m == 0:
                            C.wait(P_, ev_h)
                            if tile == 0:
                                C.wait(P_, ev_Cres2)
                                C.wait(P_, ev_Cres3)
                        first = True
                        for nch in range(2):
                            P_.matmul(pb[b][:],
                                      sb_CreT[:, nch * D + m * 128:nch * D + (m + 1) * 128],
                                      sb_hre[:, nch * T:(nch + 1) * T],
                                      start=first, stop=False)
                            first = False
                            P_.matmul(pb[b][:],
                                      sb_CimnT[:, nch * D + m * 128:nch * D + (m + 1) * 128],
                                      sb_himn[:, nch * T:(nch + 1) * T],
                                      start=False, stop=False)
                    for dc in range(8):
                        slot = dr_ring.tick()
                        C.wait(Q, dr_ring.free_ev[slot])
                        if tile == 0 and half == 0 and dc < 2:
                            C.wait(Q, ev_D[0][0])
                            C.wait(Q, ev_D[0][1])
                        eld = C.dma(Q, sb_dr[:, slot * D:(slot + 1) * D],
                                    sDT[dc * 128:(dc + 1) * 128, :],
                                    dr_ring.sems[slot])
                        C.wait(P_, eld)
                        ins = None
                        for m in ms:
                            b = 2 + (m % 4)
                            ins = P_.matmul(pb[b][:],
                                            sb_dr[:, slot * D + m * 128:slot * D + (m + 1) * 128],
                                            sb_xnf[:, dc * T:(dc + 1) * T],
                                            start=False, stop=(dc == 7))
                        dr_ring.free_ev[slot] = C.mark(ins, "sp")
                    ev_hc = dr_ring.free_ev[(dr_ring.i - 1) % dr_ring.n]
                    ev_h_cons[0] = ev_hc
                    for m in ms:
                        b = 2 + (m % 4)
                        C.wait(S_, ev_hc)
                        if m == 0:
                            C.wait(S_, ev_y_free[0])
                        ins = S_.copy(out=sb_y[:, m * T:(m + 1) * T], in_=pb[b][:])
                        ev_y[m] = C.mark(ins, "ss")
                        bank_ev[b] = ev_y[m]

            ev_sp_y = [None]
            ev_sp_xn = [None]
            ev_sp_g = [None]
            ln_tile(0, ev_x0)
            for tile in range(NT):
                if tile == 1:
                    w12_load(2)
                if tile == 0:
                    build_tables_tile0()
                else:
                    rotate_tables()
                ev_x_free[0] = ev_xn_tile[0]
                transpose_xn()
                ev_bu = bu_mms(tile)
                if tile < NT - 1:
                    ev_xl = load_x(tile + 1)
                ev_h = tile_scan_stream(tile, ev_bu)
                gu_stage(tile)
                if tile < NT - 1:
                    ln_tile(tile + 1, ev_xl)
                C.wait(G, ev_xnf[0])
                ev_sp_xn[0] = C.dma(G, sXN[tile], sb_xnf[:], "gq")
                if tile == 0:
                    ev_WD[0] = emit_wd_casts()
                wd_stage(tile)
                if tile >= 1:
                    w12_copy_store(1)
                    w12_load(1)
                C.wait(G, ev_g[7])
                ev_sp_g[0] = C.dma(G, sG[tile], sb_gA[:], "gq")
                ev_gA_free[0] = ev_sp_g[0]
                if tile == 0:
                    ev_D[0] = emit_d_casts()
                y_stage(tile, ev_h)
                if tile >= 1:
                    w12_copy_store(1)
                    w12_load(1)
                C.wait(G, ev_y[7])
                ev_sp_y[0] = C.dma(G, sY[tile], sb_y[:], "gq")
                ev_y_free[0] = ev_sp_y[0]
                ev_xnf_free[0] = ev_h_cons[0]
                ev_xnf_free[1] = ev_sp_xn[0]
            for _ in range(len(w12_jobs)):
                w12_copy_store(1)
                w12_load(1)
            ev_cast_all = (C.at("so0"), C.at("so1"))

            for nch in range(2):
                c511 = sb_cosA[:, nch * T + T - 1:nch * T + T]
                s511 = sb_sinA[:, nch * T + T - 1:nch * T + T]
                hre = sb_Hin[:, nch * 2 + 0:nch * 2 + 1]
                him = sb_Hin[:, nch * 2 + 1:nch * 2 + 2]
                V.tensor_tensor(out=sb_acc[:, 0:1], in0=c511, in1=hre, op=ALU.mult)
                V.tensor_tensor(out=sb_acc[:, 1:2], in0=s511, in1=him, op=ALU.mult)
                VF()
                V.tensor_sub(sb_S[:, nch:nch + 1], sb_acc[:, 0:1], sb_acc[:, 1:2])
                V.tensor_tensor(out=sb_acc[:, 0:1], in0=s511, in1=hre, op=ALU.mult)
                V.tensor_tensor(out=sb_acc[:, 1:2], in0=c511, in1=him, op=ALU.mult)
                VF()
                V.tensor_add(sb_S[:, 2 + nch:3 + nch],
                             sb_acc[:, 0:1], sb_acc[:, 1:2])
            VF()
            ev_S = C.at("sv")

            C.wait(G, ev_S)
            C.dma(G, cc_in.rearrange("(p c) -> p c", c=4), sb_S[:], "gq")
            C.wait(G, C.at("gq"))
            ins = G.collective_compute(
                "AllGather", ALU.bypass,
                replica_groups=[list(range(NCORE))],
                ins=[cc_in], outs=[cc_out])
            C.mark(ins, "cc")
            C.wait(G, C.at("cc"))
            ev_gath = C.dma(G, sb_gath[:].rearrange("p (j c) -> p j c", j=NCORE),
                            cc_out.rearrange("(j p c) -> p j c", j=NCORE, c=4), "cg")

            build_tables_tile0()
            V.tensor_scalar(out=sb_sc4[:, 0:2], in0=sb_lnr[:, 0:2], scalar1=512.0,
                            scalar2=-85.0, op0=ALU.mult, op1=ALU.max)
            ev = C.mark_last(V, "sv")
            C.wait(S_, ev)
            S_.activation(out=sb_P512[:, 0:2], in_=sb_sc4[:, 0:2], func=AF.Exp)
            ev = C.mark_last(S_, "ss")
            C.wait(V, ev)
            C.wait(V, ev_rot)
            V.tensor_mul(sb_P512[:, 2:4], sb_P512[:, 0:2], sb_rot[:, 2:4])
            VF()
            V.tensor_mul(sb_P512[:, 0:2], sb_P512[:, 0:2], sb_rot[:, 0:2])
            VF()
            ev_tabB = ev_tables[0]

            C.wait(V, ev_gath)
            for k in range(3):
                V.memset(sb_acc[:], 0.0)
                VF()
                cur, nxt = sb_acc, sb_acc2
                for j in range(NCORE):
                    V.scalar_tensor_tensor(out=nxt[:], in0=sb_gath[:, 4 * j:4 * j + 4],
                                           scalar=sb_msel[:, j * 3 + k:j * 3 + k + 1],
                                           in1=cur[:], op0=ALU.mult, op1=ALU.add)
                    VF()
                    cur, nxt = nxt, cur
                V.tensor_copy(out=sb_TK[:, 4 * k:4 * k + 4], in_=cur[:])
                VF()
            V.tensor_scalar(out=sb_sc4[:, 0:2], in0=sb_lnr[:, 0:2], scalar1=4096.0,
                            scalar2=-85.0, op0=ALU.mult, op1=ALU.max)
            VF()
            ev = C.mark_last(V, "sv")
            C.wait(S_, ev)
            S_.activation(out=sb_P[:, 0:2], in_=sb_sc4[:, 0:2], func=AF.Exp)
            ev = C.mark_last(S_, "ss")
            C.wait(V, ev)
            trig_cols(4096.0, sb_acc2)
            C.wait(V, ev)
            V.tensor_mul(sb_P[:, 2:4], sb_P[:, 0:2], sb_acc2[:, 2:4])
            V.tensor_mul(sb_P[:, 0:2], sb_P[:, 0:2], sb_acc2[:, 0:2])
            VF()

            def cmul4(dst0, a0, b0):
                V.tensor_mul(sb_acc[:, 0:2], a0[:, 0:2], b0[:, 0:2])
                V.tensor_mul(sb_acc[:, 2:4], a0[:, 2:4], b0[:, 2:4])
                VF()
                V.tensor_sub(dst0[:, 0:2], sb_acc[:, 0:2], sb_acc[:, 2:4])
                V.tensor_mul(sb_acc[:, 0:2], a0[:, 0:2], b0[:, 2:4])
                V.tensor_mul(sb_acc[:, 2:4], a0[:, 2:4], b0[:, 0:2])
                VF()
                V.tensor_add(dst0[:, 2:4], sb_acc[:, 0:2], sb_acc[:, 2:4])
                VF()

            cmul4(sb_P[:, 4:8], sb_P[:, 0:4], sb_P[:, 0:4])
            cmul4(sb_acc2[:], sb_P[:, 0:4], sb_TK[:, 4:8])
            V.tensor_add(sb_TK[:, 0:4], sb_TK[:, 0:4], sb_acc2[:])
            VF()
            cmul4(sb_acc2[:], sb_P[:, 4:8], sb_TK[:, 8:12])
            V.tensor_add(sb_TK[:, 0:4], sb_TK[:, 0:4], sb_acc2[:])
            VF()
            for nch in range(2):
                cth = sb_cs1[:, nch:nch + 1]
                sth = sb_cs1[:, 2 + nch:3 + nch]
                hre = sb_TK[:, nch:nch + 1]
                him = sb_TK[:, 2 + nch:3 + nch]
                V.tensor_tensor(out=sb_acc[:, 0:1], in0=cth, in1=hre, op=ALU.mult)
                V.tensor_tensor(out=sb_acc[:, 1:2], in0=sth, in1=him, op=ALU.mult)
                VF()
                V.tensor_sub(sb_acc2[:, 0:1], sb_acc[:, 0:1], sb_acc[:, 1:2])
                V.tensor_tensor(out=sb_acc[:, 0:1], in0=sth, in1=hre, op=ALU.mult)
                V.tensor_tensor(out=sb_acc[:, 1:2], in0=cth, in1=him, op=ALU.mult)
                VF()
                V.tensor_add(sb_acc2[:, 1:2], sb_acc[:, 0:1], sb_acc[:, 1:2])
                VF()
                V.tensor_scalar_mul(sb_S0[:, nch:nch + 1], sb_acc2[:, 0:1],
                                    sb_r[:, nch:nch + 1])
                V.tensor_scalar_mul(sb_S0[:, 2 + nch:3 + nch], sb_acc2[:, 1:2],
                                    sb_r[:, nch:nch + 1])
            VF()
            V.tensor_scalar_mul(sb_S0[:, 4:8], sb_S0[:, 0:4], -1.0)
            VF()

            ev_allspill = C.at("gq")

            ev_p = [None] * 8
            ev_rhs = [None]
            ev_yadd = [None] * 8
            ev_h1_last = [None]
            ev_passA_pe = C.at("sp")
            ev_xn_trans = [None]
            ev_p7_prev = [None]

            def load_x_g(tile):
                C.wait(G, ev_x_free[0])
                ev = None
                for tc in range(4):
                    ev = C.dma(G, sb_x[:, tc * D:(tc + 1) * D],
                               x_in[(tile * 4 + tc) * 128:(tile * 4 + tc + 1) * 128, :],
                               "xq")
                return ev

            def reload_y(t, ev_gate):
                C.wait(G, ev_gate)
                if t == 0:
                    C.wait(G, ev_cast_all[0])
                    C.wait(G, ev_cast_all[1])
                    C.wait(G, ev_passA_pe)
                return C.dma(G, sb_y[:], sY[t], "rly")

            def reload_xg(t, ev_gate):
                C.wait(G, ev_gate)
                C.dma(G, sb_xnf[:], sXN[t], "rlx")
                return C.dma(G, sb_gA[:], sG[t], "rlx")

            def rhs_stage(t):
                C.wait(V, ev_h_cons[0])
                if t == 0:
                    C.wait(V, ev_tabB)
                else:
                    V.tensor_mul(sb_acc[:, 0:2], sb_P512[:, 0:2], sb_S0[:, 0:2])
                    V.tensor_mul(sb_acc[:, 2:4], sb_P512[:, 2:4], sb_S0[:, 2:4])
                    V.tensor_mul(sb_acc2[:, 0:2], sb_P512[:, 0:2], sb_S0[:, 2:4])
                    V.tensor_mul(sb_acc2[:, 2:4], sb_P512[:, 2:4], sb_S0[:, 0:2])
                    VF()
                    V.tensor_sub(sb_S0[:, 0:2], sb_acc[:, 0:2], sb_acc[:, 2:4])
                    V.tensor_add(sb_S0[:, 2:4], sb_acc2[:, 0:2], sb_acc2[:, 2:4])
                    VF()
                    V.tensor_scalar_mul(sb_S0[:, 4:8], sb_S0[:, 0:4], -1.0)
                    VF()
                for nch in range(2):
                    cs = sb_cosA[:, nch * T:(nch + 1) * T]
                    sn = sb_sinA[:, nch * T:(nch + 1) * T]
                    rp = sb_rpw[:, nch * T:(nch + 1) * T]
                    V.tensor_scalar_mul(sb_tA[:], cs, sb_S0[:, nch:nch + 1])
                    V.tensor_scalar_mul(sb_tB[:], sn, sb_S0[:, 2 + nch:3 + nch])
                    V.tensor_sub(sb_tA[:], sb_tA[:], sb_tB[:])
                    V.tensor_tensor(out=sb_hre[:, nch * T:(nch + 1) * T],
                                    in0=sb_tA[:], in1=rp, op=ALU.mult)
                    V.tensor_scalar_mul(sb_tA[:], sn, sb_S0[:, 4 + nch:5 + nch])
                    V.tensor_scalar_mul(sb_tB[:], cs, sb_S0[:, 6 + nch:7 + nch])
                    V.tensor_add(sb_tA[:], sb_tA[:], sb_tB[:])
                    V.tensor_tensor(out=sb_himn[:, nch * T:(nch + 1) * T],
                                    in0=sb_tA[:], in1=rp, op=ALU.mult)
                ev_rhs[0] = C.mark_last(V, "sv")

            def corr_stage(t, ev_rly):
                for m in range(8):
                    b = m % 4
                    C.wait(P_, bank_ev[b])
                    if m == 0:
                        C.wait(P_, ev_rhs[0])
                    ins = None
                    first = True
                    for nch in range(2):
                        P_.matmul(pb[b][:],
                                  sb_CreT[:, nch * D + m * 128:nch * D + (m + 1) * 128],
                                  sb_hre[:, nch * T:(nch + 1) * T],
                                  start=first, stop=False)
                        first = False
                        ins = P_.matmul(
                            pb[b][:],
                            sb_CimnT[:, nch * D + m * 128:nch * D + (m + 1) * 128],
                            sb_himn[:, nch * T:(nch + 1) * T],
                            start=False, stop=(nch == 1))
                    evm = C.mark(ins, "sp")
                    if m == 7:
                        ev_h_cons[0] = evm
                    C.wait(V, evm)
                    if m == 0:
                        C.wait(V, ev_rly)
                    ins = V.tensor_tensor(out=sb_y[:, m * T:(m + 1) * T],
                                          in0=pb[b][:],
                                          in1=sb_y[:, m * T:(m + 1) * T],
                                          op=ALU.add)
                    ev_yadd[m] = C.mark(ins, "sv")
                    bank_ev[b] = ev_yadd[m]

            def w2_load(m):
                slot = w2_ring.tick()
                C.wait(G, w2_ring.free_ev[slot])
                ev = C.dma(G, sb_w2r[:, slot * DH1:(slot + 1) * DH1],
                           sW2[m, :, :], w2_ring.sems[slot])
                return (slot, ev)

            ev_rly = reload_y(0, ev_allspill)
            ev_rlx = reload_xg(0, None)
            ev_xl = load_x_g(0)
            rhs_stage(0)
            corr_stage(0, ev_rly)

            for tile in range(NT):
                ev_h1 = [None] * 32
                for ng in range(8):
                    bks = [(ng * 4 + nb) % 6 for nb in range(4)]
                    ins = None
                    for kk in range(0, 8, 2):
                        slots = []
                        for k in (kk, kk + 1):
                            slot = w1_ring.tick()
                            C.wait(Q, w1_ring.free_ev[slot])
                            if tile == 0 and ng == 0 and k == 0:
                                C.wait(Q, ev_cast_all[0])
                                C.wait(Q, ev_cast_all[1])
                            slots.append((k, slot,
                                          C.dma(Q, sb_w1r[:, slot * 512:(slot + 1) * 512],
                                                sW1[k * 128:(k + 1) * 128, ng * 512:(ng + 1) * 512],
                                                w1_ring.sems[slot])))
                        for k, slot, eld in slots:
                            C.wait(P_, eld)
                        if kk == 0:
                            for b in bks:
                                C.wait(P_, bank_ev[b])
                        for k, slot, eld in slots:
                            if ng == 0:
                                C.wait(P_, ev_yadd[k])
                            for nb in range(4):
                                ins = P_.matmul(
                                    pb[bks[nb]][:],
                                    sb_w1r[:, slot * 512 + nb * 128:slot * 512 + (nb + 1) * 128],
                                    sb_y[:, k * T:(k + 1) * T],
                                    start=(k == 0), stop=(k == 7))
                            w1_ring.free_ev[slot] = C.mark(ins, "sp")
                    evk = w1_ring.free_ev[(w1_ring.i - 1) % w1_ring.n]
                    for nb in range(4):
                        n = ng * 4 + nb
                        C.wait(S_, evk)
                        if n == 0:
                            C.wait(S_, ev_h1_last[0])
                        ins = S_.activation(out=sb_big[:, n * T:(n + 1) * T],
                                            in_=pb[bks[nb]][:],
                                            func=AF.Gelu_apprx_tanh)
                        ev_h1[n] = C.mark(ins, "ss")
                        bank_ev[bks[nb]] = ev_h1[n]
                ev_w1_yread = w1_ring.free_ev[(w1_ring.i - 1) % w1_ring.n]

                pre = [w2_load(0), w2_load(1)]
                if tile < NT - 1:
                    ev_rly = reload_y(tile + 1, ev_w1_yread)
                    rhs_stage(tile + 1)
                    corr_stage(tile + 1, ev_rly)

                for m in range(8):
                    slot, eld = pre[m]
                    b = 4 + (m % 2)
                    C.wait(P_, bank_ev[b])
                    C.wait(P_, eld)
                    ins = None
                    for n in range(32):
                        if n == 0:
                            C.wait(P_, ev_h1[31])
                        ins = P_.matmul(
                            pb[b][:],
                            sb_w2r[:, slot * DH1 + n * 128:slot * DH1 + (n + 1) * 128],
                            sb_big[:, n * T:(n + 1) * T],
                            start=(n == 0), stop=(n == 31))
                    w2_ring.free_ev[slot] = C.mark(ins, "sp")
                    if m == 7:
                        ev_h1_last[0] = w2_ring.free_ev[slot]
                    if m + 2 < 8:
                        pre.append(w2_load(m + 2))
                    C.wait(V, w2_ring.free_ev[slot])
                    if m == 0:
                        C.wait(V, ev_rlx)
                        C.wait(V, ev_xn_trans[0])
                    V.scalar_tensor_tensor(out=sb_tA[:],
                                           in0=pb[b][:], scalar=sb_sig[:, 0:1],
                                           in1=sb_xnf[:, m * T:(m + 1) * T],
                                           op0=ALU.mult, op1=ALU.add)
                    ev = C.mark_last(V, "sv")
                    C.wait(V, ev)
                    ins = V.tensor_tensor(out=sb_xn[:, m * T:(m + 1) * T],
                                          in0=sb_tA[:],
                                          in1=sb_gA[:, m * T:(m + 1) * T],
                                          op=ALU.mult)
                    ev_p[m] = C.mark(ins, "sv")
                    bank_ev[b] = ev_p[m]

                if tile < NT - 1:
                    ev_rlx = reload_xg(tile + 1, ev_p[7])

                ev_odma = None
                insB = None
                for tc in range(4):
                    C.wait(P_, pto_ev[0])
                    C.wait(P_, pto_ev[1])
                    for m in range(8):
                        bk = m // 4
                        C.wait(P_, ev_p[m])
                        ins = P_.transpose(pto[bk][:, (m % 4) * 128:(m % 4 + 1) * 128],
                                           sb_xn[:, m * T + tc * 128:m * T + (tc + 1) * 128],
                                           sb_identb[:])
                        if m == 3:
                            insA = ins
                        if m == 7:
                            insB = ins
                    C.mark(insA, "sp")
                    evB = C.mark(insB, "sp")
                    C.wait(V, evB)
                    if tc == 0:
                        C.wait(V, ev_xl)
                    V.tensor_tensor(out=sb_x[:, tc * D:tc * D + 512],
                                    in0=sb_x[:, tc * D:tc * D + 512],
                                    in1=pto[0][:], op=ALU.add)
                    ins = V.tensor_tensor(out=sb_x[:, tc * D + 512:(tc + 1) * D],
                                          in0=sb_x[:, tc * D + 512:(tc + 1) * D],
                                          in1=pto[1][:], op=ALU.add)
                    ev_add = C.mark(ins, "sv")
                    pto_ev[0] = ev_add
                    pto_ev[1] = ev_add
                    C.wait(G, ev_add)
                    ev_odma = C.dma(G, out_o[(tile * 4 + tc) * 128:(tile * 4 + tc + 1) * 128, :],
                                    sb_x[:, tc * D:(tc + 1) * D], "gq")

                ev_x_free[0] = ev_odma
                ev_xn_trans[0] = C.at("sp")
                if tile < NT - 1:
                    ev_xl = load_x_g(tile + 1)

            G.wait_ge(sems["gq"], C.cnt["gq"])

        @block.sync
        def _(sync):
            emit_program(Ctx(nc, "snc", sems))

        @block.gpsimd
        def _(gpsimd):
            emit_program(Ctx(nc, "gps", sems))

        @block.vector
        def _(vector):
            emit_program(Ctx(nc, "vec", sems))

        @block.scalar
        def _(scalar):
            emit_program(Ctx(nc, "scl", sems))

        @block.tensor
        def _(tensor):
            emit_program(Ctx(nc, "pe", sems))

    return nc


_NC_CACHE = [None]


def kernel(L0_attn_x3d, L0_attn_nu_log, L0_attn_theta_log, L0_attn_B_re,
           L0_attn_B_im, L0_attn_C_re, L0_attn_C_im, L0_attn_D_mat, L0_attn_W1,
           L0_attn_W2, L0_attn_res_logit, L0_ffn_w_gate, L0_ffn_w_up,
           L0_ffn_w_down):
    x = np.ascontiguousarray(np.asarray(L0_attn_x3d, np.float32).reshape(2 * 16384, D))
    BreT = np.ascontiguousarray(np.asarray(L0_attn_B_re, np.float32).T)
    BimT = np.ascontiguousarray(np.asarray(L0_attn_B_im, np.float32).T)
    CreT = np.ascontiguousarray(np.asarray(L0_attn_C_re, np.float32).T)
    CimT = np.ascontiguousarray(np.asarray(L0_attn_C_im, np.float32).T)
    DT = np.ascontiguousarray(np.asarray(L0_attn_D_mat, np.float32).T)
    W1 = np.ascontiguousarray(np.asarray(L0_attn_W1, np.float32))
    W2 = np.ascontiguousarray(np.asarray(L0_attn_W2, np.float32))
    WGT = np.ascontiguousarray(np.asarray(L0_ffn_w_gate, np.int32).T)
    WUT = np.ascontiguousarray(np.asarray(L0_ffn_w_up, np.int32).T)
    WDT = np.ascontiguousarray(np.asarray(L0_ffn_w_down, np.int32).T)
    nu = np.ascontiguousarray(np.asarray(L0_attn_nu_log, np.float32).reshape(NST, 1))
    th = np.ascontiguousarray(np.asarray(L0_attn_theta_log, np.float32).reshape(NST, 1))
    rsl = np.ascontiguousarray(
        np.broadcast_to(np.asarray(L0_attn_res_logit, np.float32).reshape(1, 1),
                        (128, 1)))

    if _NC_CACHE[0] is None:
        _NC_CACHE[0] = build_nc()
    nc = _NC_CACHE[0]

    in_maps = []
    for c in range(NCORE):
        msel = np.zeros((8, 3), np.float32)
        for k in range(3):
            j = c - 1 - k
            if j >= 0 and (j // 4) == (c // 4):
                msel[j, k] = 1.0
        msel = np.ascontiguousarray(
            np.broadcast_to(msel.reshape(1, 24), (128, 24)).astype(np.float32))
        in_maps.append(dict(
            x=np.ascontiguousarray(x[c * SEG:(c + 1) * SEG]),
            BreT=BreT, BimT=BimT, CreT=CreT, CimT=CimT, DT=DT,
            W1=W1, W2=W2, WGT=WGT, WUT=WUT, WDT=WDT,
            nu_log=nu, th_log=th, rsl=rsl, msel=msel))
    res = run_bass_kernel_spmd(nc, in_maps, core_ids=list(range(NCORE)))
    out = np.concatenate([np.asarray(r["out"], np.float32) for r in res.results],
                         axis=0)
    return out.reshape(2, 16384, D)


# version 16
# speedup vs baseline: 1.0157x; 1.0157x over previous
import numpy as np
import concourse.bass as bass
import concourse.mybir as mybir
from concourse.bass_utils import run_bass_kernel_spmd

F32 = mybir.dt.float32
BF16 = mybir.dt.bfloat16
I32 = mybir.dt.int32
AF = mybir.ActivationFunctionType
ALU = mybir.AluOpType
AX = mybir.AxisListType
PI = float(np.pi)
INV2PI = 1.0 / (2.0 * PI)

DEBUG = False
NCORE = 8
SEG = 4096
T = 512
NT = SEG // T
W = 128
NCK = T // W
D = 1024
NST = 256
DH1 = 4096
DH2 = 3072


class _Fake:
    def then_inc(self, *a, **k):
        return self


_FAKE = _Fake()


class EngProxy:
    def __init__(self, real, active):
        object.__setattr__(self, "_real", real)
        object.__setattr__(self, "_active", active)
        object.__setattr__(self, "_last", None)

    def __getattr__(self, name):
        if object.__getattribute__(self, "_active"):
            fn = getattr(object.__getattribute__(self, "_real"), name)
            if not callable(fn):
                return fn

            def wrap(*a, **k):
                r = fn(*a, **k)
                object.__setattr__(self, "_last", r)
                return r
            return wrap
        return lambda *a, **k: _FAKE


class Ctx:
    def __init__(self, nc, role, sems):
        self.nc = nc
        self.role = role
        self.sems = sems
        self.cnt = {k: 0 for k in sems}
        self.pe = EngProxy(nc.tensor, role == "pe")
        self.vec = EngProxy(nc.vector, role == "vec")
        self.scl = EngProxy(nc.scalar, role == "scl")
        self.gps = EngProxy(nc.gpsimd, role == "gps")
        self.snc = EngProxy(nc.sync, role == "snc")

    def mark_last(self, proxy, sem, n=1):
        ins = object.__getattribute__(proxy, "_last") \
            if object.__getattribute__(proxy, "_active") else None
        return self.mark(ins, sem, n)

    def mark(self, ins, sem, n=1):
        self.cnt[sem] += n
        if ins is not None and ins is not _FAKE:
            ins.then_inc(self.sems[sem], n)
        return (sem, self.cnt[sem])

    def dma(self, eng, out, in_, sem):
        ins = eng.dma_start(out=out, in_=in_)
        return self.mark(ins, sem, 16)

    def wait(self, eng, ev):
        if ev is None:
            return
        sem, val = ev
        if val > 0:
            eng.wait_ge(self.sems[sem], val)

    def at(self, sem):
        return (sem, self.cnt[sem])


class Ring:

    def __init__(self, nslots, sembase):
        self.n = nslots
        self.free_ev = [None] * nslots
        self.sems = [f"{sembase}{s}" for s in range(nslots)]
        self.i = 0

    def tick(self):
        s = self.i % self.n
        self.i += 1
        return s


def build_nc():
    nc = bass.Bass("TRN2", target_bir_lowering=False, debug=False)

    x_in = nc.dram_tensor("x", [SEG, D], F32, kind="ExternalInput").ap()
    BreT_i = nc.dram_tensor("BreT", [D, NST], F32, kind="ExternalInput").ap()
    BimT_i = nc.dram_tensor("BimT", [D, NST], F32, kind="ExternalInput").ap()
    CreT_i = nc.dram_tensor("CreT", [NST, D], F32, kind="ExternalInput").ap()
    CimT_i = nc.dram_tensor("CimT", [NST, D], F32, kind="ExternalInput").ap()
    DT_i = nc.dram_tensor("DT", [D, D], F32, kind="ExternalInput").ap()
    W1_i = nc.dram_tensor("W1", [D, DH1], F32, kind="ExternalInput").ap()
    W2_i = nc.dram_tensor("W2", [DH1, D], F32, kind="ExternalInput").ap()
    WGT_i = nc.dram_tensor("WGT", [D, DH2], I32, kind="ExternalInput").ap()
    WUT_i = nc.dram_tensor("WUT", [D, DH2], I32, kind="ExternalInput").ap()
    WDT_i = nc.dram_tensor("WDT", [DH2, D], I32, kind="ExternalInput").ap()
    nu_log = nc.dram_tensor("nu_log", [NST, 1], F32, kind="ExternalInput").ap()
    th_log = nc.dram_tensor("th_log", [NST, 1], F32, kind="ExternalInput").ap()
    rsl = nc.dram_tensor("rsl", [128, 1], F32, kind="ExternalInput").ap()
    msel_i = nc.dram_tensor("msel", [128, 24], F32, kind="ExternalInput").ap()
    out_o = nc.dram_tensor("out", [SEG, D], F32, kind="ExternalOutput").ap()

    _skind = "ExternalOutput" if DEBUG else "Internal"
    sW1 = nc.dram_tensor("sW1", [D, DH1], BF16, kind=_skind).ap()
    sGU = nc.dram_tensor("sGU", [D, 8, 768], BF16, kind=_skind).ap()
    sW2 = nc.dram_tensor("sW2", [8, 128, DH1], BF16, kind=_skind).ap()
    sWD = nc.dram_tensor("sWD", [8, 128, DH2], BF16, kind=_skind).ap()
    sBC = nc.dram_tensor("sBC", [D, 512], BF16, kind=_skind).ap()
    sDT = nc.dram_tensor("sDT", [D, D], BF16, kind=_skind).ap()
    sY = nc.dram_tensor("sY", [NT, 128, 8 * T], BF16, kind=_skind).ap()
    sXN = nc.dram_tensor("sXN", [NT, 128, 8 * T], BF16, kind=_skind).ap()
    sG = nc.dram_tensor("sG", [NT, 128, 8 * T], BF16, kind=_skind).ap()
    cc_in = nc.dram_tensor("cc_in", [512], F32).ap()
    cc_out = nc.dram_tensor("cc_out", [NCORE * 512], F32).ap()

    from contextlib import ExitStack
    es = ExitStack()
    with es:
        def sbt(name, shape, dt=F32):
            return es.enter_context(nc.sbuf_tensor(name, shape, dt))

        sb_CreT = sbt("sb_CreT", [128, 2 * D], BF16)
        sb_CimnT = sbt("sb_CimnT", [128, 2 * D], BF16)
        sb_iota = sbt("sb_iota", [128, T])
        sb_iot1 = sbt("sb_iot1", [128, W])
        sb_cosA = sbt("sb_cosA", [128, 2 * T])
        sb_sinA = sbt("sb_sinA", [128, 2 * T])
        sb_rpw = sbt("sb_rpw", [128, 2 * T])
        sb_pw = sbt("sb_pw", [128, 2 * W])
        sb_rb = sbt("sb_rb", [128, 2 * W])
        sb_rho = sbt("sb_rho", [128, 2 * NCK])
        sb_theta = sbt("sb_theta", [128, 2])
        sb_lnr = sbt("sb_lnr", [128, 2])
        sb_r = sbt("sb_r", [128, 2])
        sb_gam = sbt("sb_gam", [128, 2])
        sb_cs1 = sbt("sb_cs1", [128, 4])
        sb_rot = sbt("sb_rot", [128, 4])
        sb_P512 = sbt("sb_P512", [128, 4])
        sb_S0 = sbt("sb_S0", [128, 8])
        sb_sc4 = sbt("sb_sc4", [128, 8])
        sb_sig = sbt("sb_sig", [128, 1])
        sb_identb = sbt("sb_identb", [128, 128], BF16)
        sb_mstat = sbt("sb_mstat", [128, 32])
        sb_nstat = sbt("sb_nstat", [128, 32])
        sb_msel = sbt("sb_msel", [128, 24])
        sb_Hin = sbt("sb_Hin", [128, 4])
        sb_H = sbt("sb_H", [128, 4 * NCK])
        sb_S = sbt("sb_S", [128, 4])
        sb_gath = sbt("sb_gath", [128, 32])
        sb_TK = sbt("sb_TK", [128, 12])
        sb_P = sbt("sb_P", [128, 12])
        sb_acc = sbt("sb_acc", [128, 4])
        sb_acc2 = sbt("sb_acc2", [128, 4])
        sb_x = sbt("sb_x", [128, 4 * D])
        sb_xn = sbt("sb_xn", [128, 4 * D], BF16)
        sb_xnf = sbt("sb_xnf", [128, 8 * T], BF16)
        sb_gre = sbt("sb_gre", [128, T])
        sb_gim = sbt("sb_gim", [128, T])
        sb_bu = sbt("sb_bu", [128, 2 * T])
        sb_scr = sbt("sb_scr", [128, 2 * T])
        sb_mi = sbt("sb_mi", [128, T], I32)
        sb_tA = sbt("sb_tA", [128, T])
        sb_tB = sbt("sb_tB", [128, T])
        sb_hre = sbt("sb_hre", [128, 2 * T], BF16)
        sb_himn = sbt("sb_himn", [128, 2 * T], BF16)
        sb_y = sbt("sb_y", [128, 8 * T], BF16)
        sb_gA = sbt("sb_gA", [128, 8 * T], BF16)
        sb_big = sbt("sb_big", [128, 32 * T], BF16)
        sb_br = sbt("sb_br", [128, 2 * 512], BF16)
        sb_dr = sbt("sb_dr", [128, 2 * D], BF16)
        sb_gur = sbt("sb_gur", [128, 4 * 768], BF16)
        sb_wdr = sbt("sb_wdr", [128, 2 * DH2], BF16)
        sb_w1r = sbt("sb_w1r", [128, 4 * 512], BF16)
        sb_w2r = sbt("sb_w2r", [128, 2 * DH1], BF16)
        sb_stg = sbt("sb_stg", [128, 2 * 1024])
        sb_stgi = sbt("sb_stgi", [128, 2 * 1024], I32)
        sb_stgo = sbt("sb_stgo", [128, 2 * 1024], BF16)

        pb = [es.enter_context(nc.psum_tensor(f"pb{i}", [128, 512], F32))
              for i in range(6)]
        ptx = [es.enter_context(nc.psum_tensor(f"ptx{i}", [128, 512], BF16))
               for i in range(2)]
        pto = ptx

        sems = {}
        for name in (["xq", "pq", "rly", "rlx", "cg", "gq", "sv", "ss", "sg", "sp", "cc",
                      "st0", "st1", "si0", "si1", "so0", "so1",
                      "br0", "br1", "dr0", "dr1",
                      "gu0", "gu1", "gu2", "gu3", "wd0", "wd1",
                      "w10", "w11", "w12", "w13", "w20", "w21"]):
            sems[name] = es.enter_context(nc.semaphore(name))

        block = es.enter_context(nc.Block())

        def emit_program(C: Ctx):
            V, S_, G, P_, Q = C.vec, C.scl, C.gps, C.pe, C.snc

            def VF():
                C.wait(V, C.mark_last(V, "sv"))

            bank_ev = [None] * 6
            ptx_ev = [None] * 2
            pto_ev = ptx_ev

            C.dma(Q, sb_theta[:, 0:1], th_log[0:128, :], "pq")
            C.dma(Q, sb_theta[:, 1:2], th_log[128:256, :], "pq")
            C.dma(Q, sb_lnr[:, 0:1], nu_log[0:128, :], "pq")
            C.dma(Q, sb_lnr[:, 1:2], nu_log[128:256, :], "pq")
            C.dma(Q, sb_sig[:], rsl, "pq")
            ev_params = C.dma(Q, sb_msel[:], msel_i, "pq")

            ev_x_free = [None]

            def load_x(tile):
                C.wait(Q, ev_x_free[0])
                ev = None
                for tc in range(4):
                    ev = C.dma(Q, sb_x[:, tc * D:(tc + 1) * D],
                               x_in[(tile * 4 + tc) * 128:(tile * 4 + tc + 1) * 128, :],
                               "xq")
                return ev

            ev_x0 = load_x(0)

            G.iota(sb_iota[:], pattern=[[1, T]], base=0, channel_multiplier=0,
                   allow_small_or_imprecise_dtypes=True)
            G.iota(sb_iot1[:], pattern=[[1, W]], base=1, channel_multiplier=0,
                   allow_small_or_imprecise_dtypes=True)
            G.memset(sb_identb[:], 0.0)
            G.affine_select(out=sb_identb[:], in_=sb_identb[:],
                            compare_op=ALU.not_equal, fill=1.0, base=0,
                            pattern=[[-1, 128]], channel_multiplier=1)
            G.memset(sb_Hin[:], 0.0)
            ev_gsetup = C.mark_last(G, "sg")

            C.wait(S_, ev_params)
            S_.activation(out=sb_theta[:, 0:2], in_=sb_theta[:, 0:2], func=AF.Exp)
            S_.activation(out=sb_r[:, 0:2], in_=sb_lnr[:, 0:2], func=AF.Exp)
            S_.activation(out=sb_sig[:], in_=sb_sig[:], func=AF.Sigmoid)
            ev = C.mark_last(S_, "ss")
            C.wait(V, ev)
            V.tensor_scalar_mul(sb_lnr[:, 0:2], sb_r[:, 0:2], -1.0)
            ev = C.mark_last(V, "sv")
            C.wait(S_, ev)
            S_.activation(out=sb_r[:, 0:2], in_=sb_lnr[:, 0:2], func=AF.Exp)
            C.wait(S_, ev_gsetup)
            for nch in range(2):
                S_.activation(out=sb_pw[:, nch * W:(nch + 1) * W], in_=sb_iot1[:],
                              func=AF.Exp, scale=sb_lnr[:, nch:nch + 1])
            ev_s2 = C.mark_last(S_, "ss")
            C.wait(V, ev_s2)
            for nch in range(2):
                V.tensor_scalar(out=sb_scr[:, nch * T:(nch + 1) * T], in0=sb_iota[:],
                                scalar1=sb_lnr[:, nch:nch + 1], scalar2=-85.0,
                                op0=ALU.mult, op1=ALU.max)
            ev = C.mark_last(V, "sv")
            C.wait(S_, ev)
            for nch in range(2):
                S_.activation(out=sb_rpw[:, nch * T:(nch + 1) * T],
                              in_=sb_scr[:, nch * T:(nch + 1) * T], func=AF.Exp)
            ev_rpw = C.mark_last(S_, "ss")
            C.wait(V, ev_rpw)
            V.tensor_mul(sb_gam[:, 0:2], sb_r[:, 0:2], sb_r[:, 0:2])
            VF()
            V.tensor_scalar(out=sb_gam[:, 0:2], in0=sb_gam[:, 0:2], scalar1=-1.0,
                            scalar2=1.0, op0=ALU.mult, op1=ALU.add)
            ev = C.mark_last(V, "sv")
            C.wait(S_, ev)
            S_.activation(out=sb_gam[:, 0:2], in_=sb_gam[:, 0:2], func=AF.Sqrt)
            ev_gam = C.mark_last(S_, "ss")
            C.wait(V, ev_gam)
            V.memset(sb_rb[:], 1.0)
            V.memset(sb_rho[:], 1.0)
            V.memset(sb_sc4[:, 4:5], PI / 2.0)
            for nch in range(2):
                V.tensor_scalar_mul(sb_rb[:, nch * W:(nch + 1) * W],
                                    sb_rb[:, nch * W:(nch + 1) * W],
                                    sb_r[:, nch:nch + 1])
                V.tensor_scalar_mul(sb_rho[:, nch * NCK:(nch + 1) * NCK],
                                    sb_rho[:, nch * NCK:(nch + 1) * NCK],
                                    sb_pw[:, nch * W + W - 1:nch * W + W])
            ev_bcast = C.mark_last(V, "sv")

            def trig_cols(val, dst):
                V.tensor_scalar_mul(sb_sc4[:, 0:2], sb_theta[:, 0:2], val * INV2PI)
                VF()
                V.tensor_copy(out=sb_mi[:, 0:2], in_=sb_sc4[:, 0:2])
                VF()
                V.tensor_copy(out=sb_sc4[:, 2:4], in_=sb_mi[:, 0:2])
                VF()
                V.tensor_sub(sb_sc4[:, 0:2], sb_sc4[:, 0:2], sb_sc4[:, 2:4])
                VF()
                V.tensor_scalar_mul(sb_sc4[:, 0:2], sb_sc4[:, 0:2], PI / 2.0)
                ev = C.mark_last(V, "sv")
                C.wait(S_, ev)
                S_.activation(out=sb_sc4[:, 2:4], in_=sb_sc4[:, 0:2], func=AF.Sin)
                S_.activation(out=sb_sc4[:, 6:8], in_=sb_sc4[:, 0:2], func=AF.Sin,
                              scale=-1.0, bias=sb_sc4[:, 4:5])
                ev = C.mark_last(S_, "ss")
                C.wait(V, ev)
                V.tensor_mul(sb_sc4[:, 0:2], sb_sc4[:, 2:4], sb_sc4[:, 6:8])
                VF()
                V.tensor_scalar_mul(sb_sc4[:, 0:2], sb_sc4[:, 0:2], 2.0)
                V.tensor_mul(dst[:, 0:2], sb_sc4[:, 2:4], sb_sc4[:, 2:4])
                VF()
                V.tensor_scalar(out=dst[:, 0:2], in0=dst[:, 0:2], scalar1=-2.0,
                                scalar2=1.0, op0=ALU.mult, op1=ALU.add)
                VF()
                V.tensor_mul(dst[:, 2:4], sb_sc4[:, 0:2], dst[:, 0:2])
                VF()
                V.tensor_scalar_mul(dst[:, 2:4], dst[:, 2:4], 2.0)
                V.tensor_mul(sb_sc4[:, 2:4], sb_sc4[:, 0:2], sb_sc4[:, 0:2])
                VF()
                V.tensor_scalar(out=dst[:, 0:2], in0=sb_sc4[:, 2:4], scalar1=-2.0,
                                scalar2=1.0, op0=ALU.mult, op1=ALU.add)
                return C.mark_last(V, "sv")

            trig_cols(1.0, sb_cs1)
            ev_rot = trig_cols(512.0, sb_rot)

            ev_tables = [None]

            def build_tables_tile0():
                for nch in range(2):
                    cs = sb_cosA[:, nch * T:(nch + 1) * T]
                    sn = sb_sinA[:, nch * T:(nch + 1) * T]
                    V.tensor_scalar(out=sb_tA[:], in0=sb_iota[:],
                                    scalar1=sb_theta[:, nch:nch + 1], scalar2=INV2PI,
                                    op0=ALU.mult, op1=ALU.mult)
                    V.tensor_copy(out=sb_mi[:], in_=sb_tA[:])
                    V.tensor_copy(out=sb_tB[:], in_=sb_mi[:])
                    V.tensor_sub(sb_tA[:], sb_tA[:], sb_tB[:])
                    V.tensor_scalar_mul(sb_tA[:], sb_tA[:], PI / 2.0)
                    ev = C.mark_last(V, "sv")
                    C.wait(S_, ev)
                    S_.activation(out=sb_tB[:], in_=sb_tA[:], func=AF.Sin)
                    S_.activation(out=sb_gre[:], in_=sb_tA[:], func=AF.Sin,
                                  scale=-1.0, bias=sb_sc4[:, 4:5])
                    ev = C.mark_last(S_, "ss")
                    C.wait(V, ev)
                    V.tensor_mul(sb_tA[:], sb_tB[:], sb_gre[:])
                    V.tensor_scalar_mul(sb_tA[:], sb_tA[:], 2.0)
                    V.tensor_mul(cs, sb_tB[:], sb_tB[:])
                    V.tensor_scalar(out=cs, in0=cs, scalar1=-2.0, scalar2=1.0,
                                    op0=ALU.mult, op1=ALU.add)
                    V.tensor_mul(sn, sb_tA[:], cs)
                    V.tensor_scalar_mul(sn, sn, 2.0)
                    V.tensor_mul(sb_tB[:], sb_tA[:], sb_tA[:])
                    V.tensor_scalar(out=cs, in0=sb_tB[:], scalar1=-2.0, scalar2=1.0,
                                    op0=ALU.mult, op1=ALU.add)
                ev_tables[0] = C.mark_last(V, "sv")

            def rotate_tables():
                C.wait(V, ev_rot)
                for nch in range(2):
                    oc = sb_cosA[:, nch * T:(nch + 1) * T]
                    os_ = sb_sinA[:, nch * T:(nch + 1) * T]
                    c5 = sb_rot[:, nch:nch + 1]
                    s5 = sb_rot[:, 2 + nch:3 + nch]
                    V.tensor_scalar_mul(sb_tA[:], oc, c5)
                    V.tensor_scalar_mul(sb_tB[:], os_, s5)
                    V.tensor_sub(sb_gre[:], sb_tA[:], sb_tB[:])
                    V.tensor_scalar_mul(sb_tA[:], os_, c5)
                    V.tensor_scalar_mul(sb_tB[:], oc, s5)
                    V.tensor_add(sb_gim[:], sb_tA[:], sb_tB[:])
                    V.tensor_copy(out=oc, in_=sb_gre[:])
                    V.tensor_copy(out=os_, in_=sb_gim[:])
                ev_tables[0] = C.mark_last(V, "sv")

            cast_state = {
                0: {"stg": None, "sti": None, "sto": None},
                1: {"stg": None, "sti": None, "sto": None},
            }
            cast_ix = [0]

            def cast_job(src, dst, int_src=False, dst_sb=None, dst_view=None):
                h = cast_ix[0] % 2
                cast_ix[0] += 1
                E = S_ if h == 0 else V
                esem = "ss" if h == 0 else "sv"
                st = cast_state[h]
                ncols = src.shape[1]
                if int_src:
                    stgb = sb_stgi[:, h * 1024:h * 1024 + ncols]
                    lsem = f"si{h}"
                    C.wait(Q, st["sti"])
                else:
                    stgb = sb_stg[:, h * 1024:h * 1024 + ncols]
                    lsem = f"st{h}"
                    C.wait(Q, st["stg"])
                evl = C.dma(Q, stgb, src, lsem)
                C.wait(E, evl)
                if dst_sb is not None:
                    ins = E.tensor_copy(out=dst_sb, in_=stgb) if h == 1 else \
                        E.copy(out=dst_sb, in_=stgb)
                    evc = C.mark(ins, esem)
                    if int_src:
                        st["sti"] = evc
                    else:
                        st["stg"] = evc
                    return evc
                C.wait(E, st["sto"])
                ob = sb_stgo[:, h * 1024:h * 1024 + ncols]
                ins = E.tensor_copy(out=ob, in_=stgb) if h == 1 else \
                    E.copy(out=ob, in_=stgb)
                evc = C.mark(ins, esem)
                if int_src:
                    st["sti"] = evc
                else:
                    st["stg"] = evc
                C.wait(G, evc)
                if dst_view is not None:
                    evs = C.dma(G, dst, dst_view(ob), f"so{h}")
                else:
                    evs = C.dma(G, dst, ob, f"so{h}")
                st["sto"] = evs
                return evs

            ev_Cres = None
            for nch in range(2):
                cast_job(CreT_i[nch * 128:(nch + 1) * 128, :],
                         None, dst_sb=sb_CreT[:, nch * D:(nch + 1) * D])
                ev_Cres = cast_job(CimT_i[nch * 128:(nch + 1) * 128, :],
                                   None, dst_sb=sb_CimnT[:, nch * D:(nch + 1) * D])
            ev_Cres2 = C.at("ss")
            ev_Cres3 = C.at("sv")

            for dc in range(8):
                cast_job(BreT_i[dc * 128:(dc + 1) * 128, :],
                         sBC[dc * 128:(dc + 1) * 128, 0:256])
                cast_job(BimT_i[dc * 128:(dc + 1) * 128, :],
                         sBC[dc * 128:(dc + 1) * 128, 256:512])
            ev_B = (C.at("so0"), C.at("so1"))

            cast_GU = {}

            def emit_gu_casts(jg):
                for k in range(8):
                    for which, srcm in (("g", WGT_i), ("u", WUT_i)):
                        hoff = 0 if which == "g" else 384
                        src = srcm[k * 128:(k + 1) * 128, jg * 768:(jg + 1) * 768]
                        dst = sGU[k * 128:(k + 1) * 128, 2 * jg:2 * jg + 2, hoff:hoff + 384]
                        ev = cast_job(src, dst, int_src=True,
                                      dst_view=lambda ob: ob.rearrange(
                                          "p (g c) -> p g c", c=384))
                        cast_GU[(k, jg, which)] = ev

            def emit_wd_casts():
                for n in range(24):
                    cast_job(WDT_i[n * 128:(n + 1) * 128, :],
                             sWD[:, :, n * 128:(n + 1) * 128].rearrange(
                                 "m p c -> p m c"),
                             int_src=True,
                             dst_view=lambda ob: ob.rearrange(
                                 "p (m c) -> p m c", m=8))
                return (C.at("so0"), C.at("so1"))

            def emit_d_casts():
                for dc in range(8):
                    cast_job(DT_i[dc * 128:(dc + 1) * 128, :],
                             sDT[dc * 128:(dc + 1) * 128, :])
                return (C.at("so0"), C.at("so1"))

            w12_jobs = []
            for rc in range(8):
                for cp in range(4):
                    w12_jobs.append((W1_i[rc * 128:(rc + 1) * 128,
                                          cp * 1024:(cp + 1) * 1024],
                                     sW1[rc * 128:(rc + 1) * 128,
                                         cp * 1024:(cp + 1) * 1024], False))
            for n in range(32):
                w12_jobs.append((W2_i[n * 128:(n + 1) * 128, :],
                                 sW2[:, :, n * 128:(n + 1) * 128].rearrange(
                                     "m p c -> p m c"), True))
            w12_li = [0]
            w12_ci = [0]
            w12_ld = {}
            w12_stg = {0: None, 1: None}
            w12_sto = {0: None, 1: None}

            def w12_load(n):
                for _ in range(n):
                    j = w12_li[0]
                    if j >= len(w12_jobs):
                        return
                    w12_li[0] = j + 1
                    h = j % 2
                    if w12_stg[h] is None:
                        w12_stg[h] = cast_state[h]["stg"]
                    C.wait(Q, w12_stg[h])
                    src = w12_jobs[j][0]
                    w12_ld[j] = C.dma(Q, sb_stg[:, h * 1024:(h + 1) * 1024],
                                      src, f"st{h}")

            def w12_copy_store(n):
                for _ in range(n):
                    j = w12_ci[0]
                    if j >= min(w12_li[0], len(w12_jobs)):
                        return
                    w12_ci[0] = j + 1
                    h = j % 2
                    if w12_sto[h] is None:
                        w12_sto[h] = cast_state[h]["sto"]
                    C.wait(S_, w12_ld[j])
                    C.wait(S_, w12_sto[h])
                    ins = S_.copy(out=sb_stgo[:, h * 1024:(h + 1) * 1024],
                                  in_=sb_stg[:, h * 1024:(h + 1) * 1024])
                    evc = C.mark(ins, "ss")
                    w12_stg[h] = evc
                    C.wait(Q, evc)
                    src, dst, rearr = w12_jobs[j]
                    ob = sb_stgo[:, h * 1024:(h + 1) * 1024]
                    if rearr:
                        ob = ob.rearrange("p (m c) -> p m c", m=8)
                    w12_sto[h] = C.dma(Q, dst, ob, f"so{h}")

            ev_xn_tile = [None]

            def ln_tile(tile, ev_xload):
                C.wait(V, ev_xload)
                b4 = slice(tile * 4, tile * 4 + 4)
                for tc in range(4):
                    xa = sb_x[:, tc * D:(tc + 1) * D]
                    sc = slice(tile * 4 + tc, tile * 4 + tc + 1)
                    V.tensor_reduce(sb_mstat[:, sc], xa, axis=AX.X, op=ALU.add)
                VF()
                V.tensor_scalar_mul(sb_mstat[:, b4], sb_mstat[:, b4], 1.0 / D)
                VF()
                for tc in range(4):
                    xa = sb_x[:, tc * D:(tc + 1) * D]
                    sc = slice(tile * 4 + tc, tile * 4 + tc + 1)
                    V.scalar_tensor_tensor(out=sb_scr[:, 0:D], in0=xa,
                                           scalar=sb_mstat[:, sc], in1=xa,
                                           op0=ALU.subtract, op1=ALU.mult,
                                           accum_out=sb_nstat[:, sc])
                VF()
                V.tensor_scalar(out=sb_nstat[:, b4], in0=sb_nstat[:, b4],
                                scalar1=1.0 / D, scalar2=1e-5,
                                op0=ALU.mult, op1=ALU.add)
                ev = C.mark_last(V, "sv")
                C.wait(S_, ev)
                S_.activation(out=sb_nstat[:, b4], in_=sb_nstat[:, b4],
                              func=AF.Sqrt)
                ev = C.mark_last(S_, "ss")
                C.wait(V, ev)
                V.reciprocal(out=sb_nstat[:, b4], in_=sb_nstat[:, b4])
                VF()
                for tc in range(4):
                    xa = sb_x[:, tc * D:(tc + 1) * D]
                    sc = slice(tile * 4 + tc, tile * 4 + tc + 1)
                    V.tensor_scalar(out=sb_xn[:, tc * D:(tc + 1) * D], in0=xa,
                                    scalar1=sb_mstat[:, sc], scalar2=sb_nstat[:, sc],
                                    op0=ALU.subtract, op1=ALU.mult)
                ev_xn_tile[0] = C.mark_last(V, "sv")

            ev_xnf = [None]
            ev_xnf_free = [None, None]

            def transpose_xn():
                C.wait(P_, ev_xn_tile[0])
                for dc in range(8):
                    b = dc % 2
                    C.wait(P_, ptx_ev[b])
                    ins = None
                    for tc in range(4):
                        ins = P_.transpose(ptx[b][:, tc * 128:(tc + 1) * 128],
                                           sb_xn[:, tc * D + dc * 128:tc * D + (dc + 1) * 128],
                                           sb_identb[:])
                    ev = C.mark(ins, "sp")
                    C.wait(S_, ev)
                    if dc < 2:
                        C.wait(S_, ev_xnf_free[0])
                        C.wait(S_, ev_xnf_free[1])
                    ins = S_.copy(out=sb_xnf[:, dc * T:(dc + 1) * T], in_=ptx[b][:])
                    ptx_ev[b] = C.mark(ins, "ss")
                ev_xnf[0] = ptx_ev[1]

            br_ring = Ring(2, "br")

            def bu_mms(tile):
                for b in range(4):
                    C.wait(P_, bank_ev[b])
                ins = None
                for dc in range(8):
                    slot = br_ring.tick()
                    C.wait(Q, br_ring.free_ev[slot])
                    if tile == 0 and dc < 2:
                        C.wait(Q, ev_B[0])
                        C.wait(Q, ev_B[1])
                    eld = C.dma(Q, sb_br[:, slot * 512:(slot + 1) * 512],
                                sBC[dc * 128:(dc + 1) * 128, :],
                                br_ring.sems[slot])
                    C.wait(P_, eld)
                    if dc == 0:
                        C.wait(P_, ev_xnf[0])
                    rhs = sb_xnf[:, dc * T:(dc + 1) * T]
                    for nch in range(2):
                        P_.matmul(pb[nch][:],
                                  sb_br[:, slot * 512 + nch * 128:slot * 512 + (nch + 1) * 128],
                                  rhs, start=(dc == 0), stop=(dc == 7))
                        ins = P_.matmul(
                            pb[2 + nch][:],
                            sb_br[:, slot * 512 + 256 + nch * 128:slot * 512 + 256 + (nch + 1) * 128],
                            rhs, start=(dc == 0), stop=(dc == 7))
                    br_ring.free_ev[slot] = C.mark(ins, "sp")
                return br_ring.free_ev[(br_ring.i - 1) % br_ring.n]

            ev_h_cons = [None]

            def tile_scan_stream(tile, ev_bu):
                C.wait(V, ev_bu)
                C.wait(V, ev_tables[0])
                C.wait(V, ev_h_cons[0])
                V.tensor_copy(out=sb_bu[:, 0:T], in_=pb[1][:])
                ins = V.tensor_copy(out=sb_bu[:, T:2 * T], in_=pb[3][:])
                ev_bu1 = C.mark(ins, "sv")
                bank_ev[1] = ev_bu1
                bank_ev[3] = ev_bu1
                for nch in range(2):
                    pre_b = pb[nch][:] if nch == 0 else sb_bu[:, 0:T]
                    pim_b = pb[2 + nch][:] if nch == 0 else sb_bu[:, T:2 * T]
                    cs = sb_cosA[:, nch * T:(nch + 1) * T]
                    sn = sb_sinA[:, nch * T:(nch + 1) * T]
                    gm = sb_gam[:, nch:nch + 1]
                    V.tensor_tensor(out=sb_tA[:], in0=cs, in1=pre_b, op=ALU.mult)
                    V.tensor_tensor(out=sb_tB[:], in0=sn, in1=pim_b, op=ALU.mult)
                    V.tensor_add(sb_tA[:], sb_tA[:], sb_tB[:])
                    V.tensor_scalar_mul(sb_gre[:], sb_tA[:], gm)
                    V.tensor_tensor(out=sb_tA[:], in0=cs, in1=pim_b, op=ALU.mult)
                    V.tensor_tensor(out=sb_tB[:], in0=sn, in1=pre_b, op=ALU.mult)
                    V.tensor_sub(sb_tA[:], sb_tA[:], sb_tB[:])
                    ins = V.tensor_scalar_mul(sb_gim[:], sb_tA[:], gm)
                    ev_mod = C.mark(ins, "sv")
                    if nch == 0:
                        bank_ev[0] = ev_mod
                        bank_ev[2] = ev_mod
                    ins = None
                    for comp in range(2):
                        gbuf = sb_gre if comp == 0 else sb_gim
                        ubuf = sb_scr[:, comp * T:(comp + 1) * T]
                        for c in range(NCK):
                            ins = V.tensor_tensor_scan(
                                out=ubuf[:, c * W:(c + 1) * W],
                                data0=sb_rb[:, nch * W:(nch + 1) * W],
                                data1=gbuf[:, c * W:(c + 1) * W],
                                initial=0.0, op0=ALU.mult, op1=ALU.add)
                    ev = C.mark(ins, "sv")
                    C.wait(V, ev)
                    for comp in range(2):
                        ubuf = sb_scr[:, comp * T:(comp + 1) * T]
                        hcol = sb_H[:, (nch * 2 + comp) * NCK:(nch * 2 + comp + 1) * NCK]
                        ins = V.tensor_tensor_scan(
                            out=hcol,
                            data0=sb_rho[:, nch * NCK:(nch + 1) * NCK],
                            data1=ubuf[:, W - 1:T:W],
                            initial=sb_Hin[:, nch * 2 + comp:nch * 2 + comp + 1],
                            op0=ALU.mult, op1=ALU.add)
                    ev = C.mark(ins, "sv")
                    C.wait(V, ev)
                    for comp in range(2):
                        ubuf = sb_scr[:, comp * T:(comp + 1) * T]
                        hcol = sb_H[:, (nch * 2 + comp) * NCK:(nch * 2 + comp + 1) * NCK]
                        for c in range(NCK):
                            scal = (sb_Hin[:, nch * 2 + comp:nch * 2 + comp + 1]
                                    if c == 0 else hcol[:, c - 1:c])
                            V.scalar_tensor_tensor(
                                out=ubuf[:, c * W:(c + 1) * W],
                                in0=sb_pw[:, nch * W:(nch + 1) * W],
                                scalar=scal,
                                in1=ubuf[:, c * W:(c + 1) * W],
                                op0=ALU.mult, op1=ALU.add)
                    ure = sb_scr[:, 0:T]
                    uim = sb_scr[:, T:2 * T]
                    V.tensor_tensor(out=sb_tA[:], in0=cs, in1=ure, op=ALU.mult)
                    V.tensor_tensor(out=sb_tB[:], in0=sn, in1=uim, op=ALU.mult)
                    V.tensor_sub(sb_hre[:, nch * T:(nch + 1) * T], sb_tA[:], sb_tB[:])
                    V.tensor_tensor(out=sb_tA[:], in0=sn, in1=ure, op=ALU.mult)
                    V.tensor_tensor(out=sb_tB[:], in0=cs, in1=uim, op=ALU.mult)
                    V.tensor_add(sb_tA[:], sb_tA[:], sb_tB[:])
                    V.tensor_scalar_mul(sb_himn[:, nch * T:(nch + 1) * T],
                                        sb_tA[:], -1.0)
                    ins = None
                    for comp in range(2):
                        hcol = sb_H[:, (nch * 2 + comp) * NCK:(nch * 2 + comp + 1) * NCK]
                        ins = V.tensor_copy(
                            out=sb_Hin[:, nch * 2 + comp:nch * 2 + comp + 1],
                            in_=hcol[:, NCK - 1:NCK])
                    ev = C.mark(ins, "sv")
                    C.wait(V, ev)
                return C.at("sv")

            gu_ring = Ring(4, "gu")
            wd_ring = Ring(2, "wd")
            dr_ring = Ring(2, "dr")
            w1_ring = Ring(4, "w1")
            w2_ring = Ring(2, "w2")

            ev_hf = [None] * 24
            ev_g = [None] * 8
            ev_gA_free = [None]
            ev_y_free = [None]
            ev_WD = [None]
            ev_D = [None]

            def gu_stage(tile):
                for ng in range(8):
                    if tile == 0:
                        emit_gu_casts(ng // 2) if ng % 2 == 0 else None
                    insg = insu = None
                    for kk in range(0, 8, 2):
                        slots = []
                        for k in (kk, kk + 1):
                            slot = gu_ring.tick()
                            C.wait(G, gu_ring.free_ev[slot])
                            if tile == 0:
                                C.wait(G, cast_GU[(k, ng // 2, "g")])
                                C.wait(G, cast_GU[(k, ng // 2, "u")])
                            slots.append((k, slot,
                                          C.dma(G, sb_gur[:, slot * 768:(slot + 1) * 768],
                                                sGU[k * 128:(k + 1) * 128, ng, :],
                                                gu_ring.sems[slot])))
                        for k, slot, eld in slots:
                            C.wait(P_, eld)
                        if kk == 0:
                            for nb in range(6):
                                C.wait(P_, bank_ev[nb])
                        for k, slot, eld in slots:
                            for nb in range(3):
                                insg = P_.matmul(
                                    pb[nb][:],
                                    sb_gur[:, slot * 768 + nb * 128:slot * 768 + (nb + 1) * 128],
                                    sb_xnf[:, k * T:(k + 1) * T],
                                    start=(k == 0), stop=(k == 7))
                                insu = P_.matmul(
                                    pb[3 + nb][:],
                                    sb_gur[:, slot * 768 + 384 + nb * 128:slot * 768 + 384 + (nb + 1) * 128],
                                    sb_xnf[:, k * T:(k + 1) * T],
                                    start=(k == 0), stop=(k == 7))
                            gu_ring.free_ev[slot] = C.mark(insu, "sp")
                    evk = gu_ring.free_ev[(gu_ring.i - 1) % gu_ring.n]
                    for nb in range(3):
                        n = ng * 3 + nb
                        C.wait(S_, evk)
                        ins = S_.activation(out=sb_big[:, n * T:(n + 1) * T],
                                            in_=pb[nb][:], func=AF.Silu,
                                            scale=1.0 / 127.0)
                        evsl = C.mark(ins, "ss")
                        bank_ev[nb] = evsl
                        C.wait(V, evsl)
                        ins = V.tensor_tensor(out=sb_big[:, n * T:(n + 1) * T],
                                              in0=sb_big[:, n * T:(n + 1) * T],
                                              in1=pb[3 + nb][:], op=ALU.mult)
                        ev_hf[n] = C.mark(ins, "sv")
                        bank_ev[3 + nb] = ev_hf[n]
                    if tile >= 1:
                        w12_copy_store(1)
                        w12_load(1)

            def wd_stage(tile):
                for m in range(8):
                    b = m % 2
                    C.wait(P_, bank_ev[b])
                    slot = wd_ring.tick()
                    C.wait(G, wd_ring.free_ev[slot])
                    if tile == 0 and m < 2:
                        C.wait(G, ev_WD[0][0])
                        C.wait(G, ev_WD[0][1])
                    eld = C.dma(G, sb_wdr[:, slot * DH2:(slot + 1) * DH2],
                                sWD[m, :, :], wd_ring.sems[slot])
                    C.wait(P_, eld)
                    ins = None
                    for n in range(24):
                        if n == 0:
                            C.wait(P_, ev_hf[23])
                        ins = P_.matmul(
                            pb[b][:],
                            sb_wdr[:, slot * DH2 + n * 128:slot * DH2 + (n + 1) * 128],
                            sb_big[:, n * T:(n + 1) * T],
                            start=(n == 0), stop=(n == 23))
                    wd_ring.free_ev[slot] = C.mark(ins, "sp")
                    C.wait(S_, wd_ring.free_ev[slot])
                    if m == 0:
                        C.wait(S_, ev_gA_free[0])
                    ins = S_.activation(out=sb_gA[:, m * T:(m + 1) * T],
                                        in_=pb[b][:], func=AF.Silu,
                                        scale=1.0 / (127.0 * 127.0))
                    ev_g[m] = C.mark(ins, "ss")
                    bank_ev[b] = ev_g[m]

            ev_y = [None] * 8

            def y_stage(tile, ev_h):
                for half in range(2):
                    ms = range(4 * half, 4 * half + 4)
                    for m in ms:
                        b = 2 + (m % 4)
                        C.wait(P_, bank_ev[b])
                        if m == 0:
                            C.wait(P_, ev_h)
                            if tile == 0:
                                C.wait(P_, ev_Cres2)
                                C.wait(P_, ev_Cres3)
                        first = True
                        for nch in range(2):
                            P_.matmul(pb[b][:],
                                      sb_CreT[:, nch * D + m * 128:nch * D + (m + 1) * 128],
                                      sb_hre[:, nch * T:(nch + 1) * T],
                                      start=first, stop=False)
                            first = False
                            P_.matmul(pb[b][:],
                                      sb_CimnT[:, nch * D + m * 128:nch * D + (m + 1) * 128],
                                      sb_himn[:, nch * T:(nch + 1) * T],
                                      start=False, stop=False)
                    for dc in range(8):
                        slot = dr_ring.tick()
                        C.wait(Q, dr_ring.free_ev[slot])
                        if tile == 0 and half == 0 and dc < 2:
                            C.wait(Q, ev_D[0][0])
                            C.wait(Q, ev_D[0][1])
                        eld = C.dma(Q, sb_dr[:, slot * D:(slot + 1) * D],
                                    sDT[dc * 128:(dc + 1) * 128, :],
                                    dr_ring.sems[slot])
                        C.wait(P_, eld)
                        ins = None
                        for m in ms:
                            b = 2 + (m % 4)
                            ins = P_.matmul(pb[b][:],
                                            sb_dr[:, slot * D + m * 128:slot * D + (m + 1) * 128],
                                            sb_xnf[:, dc * T:(dc + 1) * T],
                                            start=False, stop=(dc == 7))
                        dr_ring.free_ev[slot] = C.mark(ins, "sp")
                    ev_hc = dr_ring.free_ev[(dr_ring.i - 1) % dr_ring.n]
                    ev_h_cons[0] = ev_hc
                    for m in ms:
                        b = 2 + (m % 4)
                        C.wait(S_, ev_hc)
                        if m == 0:
                            C.wait(S_, ev_y_free[0])
                        ins = S_.copy(out=sb_y[:, m * T:(m + 1) * T], in_=pb[b][:])
                        ev_y[m] = C.mark(ins, "ss")
                        bank_ev[b] = ev_y[m]

            ev_sp_y = [None]
            ev_sp_xn = [None]
            ev_sp_g = [None]
            ln_tile(0, ev_x0)
            for tile in range(NT):
                if tile == 1:
                    w12_load(2)
                if tile == 0:
                    build_tables_tile0()
                else:
                    rotate_tables()
                ev_x_free[0] = ev_xn_tile[0]
                transpose_xn()
                ev_bu = bu_mms(tile)
                if tile < NT - 1:
                    ev_xl = load_x(tile + 1)
                ev_h = tile_scan_stream(tile, ev_bu)
                gu_stage(tile)
                if tile < NT - 1:
                    ln_tile(tile + 1, ev_xl)
                C.wait(G, ev_xnf[0])
                ev_sp_xn[0] = C.dma(G, sXN[tile], sb_xnf[:], "gq")
                if tile == 0:
                    ev_WD[0] = emit_wd_casts()
                wd_stage(tile)
                if tile >= 1:
                    w12_copy_store(1)
                    w12_load(1)
                C.wait(G, ev_g[7])
                ev_sp_g[0] = C.dma(G, sG[tile], sb_gA[:], "gq")
                ev_gA_free[0] = ev_sp_g[0]
                if tile == 0:
                    ev_D[0] = emit_d_casts()
                y_stage(tile, ev_h)
                if tile >= 1:
                    w12_copy_store(1)
                    w12_load(1)
                C.wait(G, ev_y[7])
                ev_sp_y[0] = C.dma(G, sY[tile], sb_y[:], "gq")
                ev_y_free[0] = ev_sp_y[0]
                ev_xnf_free[0] = ev_h_cons[0]
                ev_xnf_free[1] = ev_sp_xn[0]
            for _ in range(len(w12_jobs)):
                w12_copy_store(1)
                w12_load(1)
            ev_cast_all = (C.at("so0"), C.at("so1"))

            for nch in range(2):
                c511 = sb_cosA[:, nch * T + T - 1:nch * T + T]
                s511 = sb_sinA[:, nch * T + T - 1:nch * T + T]
                hre = sb_Hin[:, nch * 2 + 0:nch * 2 + 1]
                him = sb_Hin[:, nch * 2 + 1:nch * 2 + 2]
                V.tensor_tensor(out=sb_acc[:, 0:1], in0=c511, in1=hre, op=ALU.mult)
                V.tensor_tensor(out=sb_acc[:, 1:2], in0=s511, in1=him, op=ALU.mult)
                VF()
                V.tensor_sub(sb_S[:, nch:nch + 1], sb_acc[:, 0:1], sb_acc[:, 1:2])
                V.tensor_tensor(out=sb_acc[:, 0:1], in0=s511, in1=hre, op=ALU.mult)
                V.tensor_tensor(out=sb_acc[:, 1:2], in0=c511, in1=him, op=ALU.mult)
                VF()
                V.tensor_add(sb_S[:, 2 + nch:3 + nch],
                             sb_acc[:, 0:1], sb_acc[:, 1:2])
            VF()
            ev_S = C.at("sv")

            C.wait(G, ev_S)
            C.dma(G, cc_in.rearrange("(p c) -> p c", c=4), sb_S[:], "gq")
            C.wait(G, C.at("gq"))
            ins = G.collective_compute(
                "AllGather", ALU.bypass,
                replica_groups=[list(range(NCORE))],
                ins=[cc_in], outs=[cc_out])
            C.mark(ins, "cc")
            C.wait(G, C.at("cc"))
            ev_gath = C.dma(G, sb_gath[:].rearrange("p (j c) -> p j c", j=NCORE),
                            cc_out.rearrange("(j p c) -> p j c", j=NCORE, c=4), "cg")

            build_tables_tile0()
            V.tensor_scalar(out=sb_sc4[:, 0:2], in0=sb_lnr[:, 0:2], scalar1=512.0,
                            scalar2=-85.0, op0=ALU.mult, op1=ALU.max)
            ev = C.mark_last(V, "sv")
            C.wait(S_, ev)
            S_.activation(out=sb_P512[:, 0:2], in_=sb_sc4[:, 0:2], func=AF.Exp)
            ev = C.mark_last(S_, "ss")
            C.wait(V, ev)
            C.wait(V, ev_rot)
            V.tensor_mul(sb_P512[:, 2:4], sb_P512[:, 0:2], sb_rot[:, 2:4])
            VF()
            V.tensor_mul(sb_P512[:, 0:2], sb_P512[:, 0:2], sb_rot[:, 0:2])
            VF()
            ev_tabB = ev_tables[0]

            C.wait(V, ev_gath)
            for k in range(3):
                V.memset(sb_acc[:], 0.0)
                VF()
                cur, nxt = sb_acc, sb_acc2
                for j in range(NCORE):
                    V.scalar_tensor_tensor(out=nxt[:], in0=sb_gath[:, 4 * j:4 * j + 4],
                                           scalar=sb_msel[:, j * 3 + k:j * 3 + k + 1],
                                           in1=cur[:], op0=ALU.mult, op1=ALU.add)
                    VF()
                    cur, nxt = nxt, cur
                V.tensor_copy(out=sb_TK[:, 4 * k:4 * k + 4], in_=cur[:])
                VF()
            V.tensor_scalar(out=sb_sc4[:, 0:2], in0=sb_lnr[:, 0:2], scalar1=4096.0,
                            scalar2=-85.0, op0=ALU.mult, op1=ALU.max)
            VF()
            ev = C.mark_last(V, "sv")
            C.wait(S_, ev)
            S_.activation(out=sb_P[:, 0:2], in_=sb_sc4[:, 0:2], func=AF.Exp)
            ev = C.mark_last(S_, "ss")
            C.wait(V, ev)
            trig_cols(4096.0, sb_acc2)
            C.wait(V, ev)
            V.tensor_mul(sb_P[:, 2:4], sb_P[:, 0:2], sb_acc2[:, 2:4])
            V.tensor_mul(sb_P[:, 0:2], sb_P[:, 0:2], sb_acc2[:, 0:2])
            VF()

            def cmul4(dst0, a0, b0):
                V.tensor_mul(sb_acc[:, 0:2], a0[:, 0:2], b0[:, 0:2])
                V.tensor_mul(sb_acc[:, 2:4], a0[:, 2:4], b0[:, 2:4])
                VF()
                V.tensor_sub(dst0[:, 0:2], sb_acc[:, 0:2], sb_acc[:, 2:4])
                V.tensor_mul(sb_acc[:, 0:2], a0[:, 0:2], b0[:, 2:4])
                V.tensor_mul(sb_acc[:, 2:4], a0[:, 2:4], b0[:, 0:2])
                VF()
                V.tensor_add(dst0[:, 2:4], sb_acc[:, 0:2], sb_acc[:, 2:4])
                VF()

            cmul4(sb_P[:, 4:8], sb_P[:, 0:4], sb_P[:, 0:4])
            cmul4(sb_acc2[:], sb_P[:, 0:4], sb_TK[:, 4:8])
            V.tensor_add(sb_TK[:, 0:4], sb_TK[:, 0:4], sb_acc2[:])
            VF()
            cmul4(sb_acc2[:], sb_P[:, 4:8], sb_TK[:, 8:12])
            V.tensor_add(sb_TK[:, 0:4], sb_TK[:, 0:4], sb_acc2[:])
            VF()
            for nch in range(2):
                cth = sb_cs1[:, nch:nch + 1]
                sth = sb_cs1[:, 2 + nch:3 + nch]
                hre = sb_TK[:, nch:nch + 1]
                him = sb_TK[:, 2 + nch:3 + nch]
                V.tensor_tensor(out=sb_acc[:, 0:1], in0=cth, in1=hre, op=ALU.mult)
                V.tensor_tensor(out=sb_acc[:, 1:2], in0=sth, in1=him, op=ALU.mult)
                VF()
                V.tensor_sub(sb_acc2[:, 0:1], sb_acc[:, 0:1], sb_acc[:, 1:2])
                V.tensor_tensor(out=sb_acc[:, 0:1], in0=sth, in1=hre, op=ALU.mult)
                V.tensor_tensor(out=sb_acc[:, 1:2], in0=cth, in1=him, op=ALU.mult)
                VF()
                V.tensor_add(sb_acc2[:, 1:2], sb_acc[:, 0:1], sb_acc[:, 1:2])
                VF()
                V.tensor_scalar_mul(sb_S0[:, nch:nch + 1], sb_acc2[:, 0:1],
                                    sb_r[:, nch:nch + 1])
                V.tensor_scalar_mul(sb_S0[:, 2 + nch:3 + nch], sb_acc2[:, 1:2],
                                    sb_r[:, nch:nch + 1])
            VF()
            V.tensor_scalar_mul(sb_S0[:, 4:8], sb_S0[:, 0:4], -1.0)
            VF()

            ev_allspill = C.at("gq")

            ev_p = [None] * 8
            ev_rhs = [None]
            ev_yadd = [None] * 8
            ev_h1_last = [None]
            ev_passA_pe = C.at("sp")
            ev_xn_trans = [None]
            ev_p7_prev = [None]

            def load_x_g(tile):
                C.wait(G, ev_x_free[0])
                ev = None
                for tc in range(4):
                    ev = C.dma(G, sb_x[:, tc * D:(tc + 1) * D],
                               x_in[(tile * 4 + tc) * 128:(tile * 4 + tc + 1) * 128, :],
                               "xq")
                return ev

            def reload_y(t, ev_gate):
                C.wait(G, ev_gate)
                if t == 0:
                    C.wait(G, ev_cast_all[0])
                    C.wait(G, ev_cast_all[1])
                    C.wait(G, ev_passA_pe)
                return C.dma(G, sb_y[:], sY[t], "rly")

            def reload_xg(t, ev_gate):
                C.wait(G, ev_gate)
                C.dma(G, sb_xnf[:], sXN[t], "rlx")
                return C.dma(G, sb_gA[:], sG[t], "rlx")

            def rhs_stage(t):
                C.wait(V, ev_h_cons[0])
                if t == 0:
                    C.wait(V, ev_tabB)
                else:
                    V.tensor_mul(sb_acc[:, 0:2], sb_P512[:, 0:2], sb_S0[:, 0:2])
                    V.tensor_mul(sb_acc[:, 2:4], sb_P512[:, 2:4], sb_S0[:, 2:4])
                    V.tensor_mul(sb_acc2[:, 0:2], sb_P512[:, 0:2], sb_S0[:, 2:4])
                    V.tensor_mul(sb_acc2[:, 2:4], sb_P512[:, 2:4], sb_S0[:, 0:2])
                    VF()
                    V.tensor_sub(sb_S0[:, 0:2], sb_acc[:, 0:2], sb_acc[:, 2:4])
                    V.tensor_add(sb_S0[:, 2:4], sb_acc2[:, 0:2], sb_acc2[:, 2:4])
                    VF()
                    V.tensor_scalar_mul(sb_S0[:, 4:8], sb_S0[:, 0:4], -1.0)
                    VF()
                for nch in range(2):
                    cs = sb_cosA[:, nch * T:(nch + 1) * T]
                    sn = sb_sinA[:, nch * T:(nch + 1) * T]
                    rp = sb_rpw[:, nch * T:(nch + 1) * T]
                    V.tensor_scalar_mul(sb_tA[:], cs, sb_S0[:, nch:nch + 1])
                    V.tensor_scalar_mul(sb_tB[:], sn, sb_S0[:, 2 + nch:3 + nch])
                    V.tensor_sub(sb_tA[:], sb_tA[:], sb_tB[:])
                    V.tensor_tensor(out=sb_hre[:, nch * T:(nch + 1) * T],
                                    in0=sb_tA[:], in1=rp, op=ALU.mult)
                    V.tensor_scalar_mul(sb_tA[:], sn, sb_S0[:, 4 + nch:5 + nch])
                    V.tensor_scalar_mul(sb_tB[:], cs, sb_S0[:, 6 + nch:7 + nch])
                    V.tensor_add(sb_tA[:], sb_tA[:], sb_tB[:])
                    V.tensor_tensor(out=sb_himn[:, nch * T:(nch + 1) * T],
                                    in0=sb_tA[:], in1=rp, op=ALU.mult)
                ev_rhs[0] = C.mark_last(V, "sv")

            def corr_stage(t, ev_rly):
                for m in range(8):
                    b = m % 4
                    C.wait(P_, bank_ev[b])
                    if m == 0:
                        C.wait(P_, ev_rhs[0])
                    ins = None
                    first = True
                    for nch in range(2):
                        P_.matmul(pb[b][:],
                                  sb_CreT[:, nch * D + m * 128:nch * D + (m + 1) * 128],
                                  sb_hre[:, nch * T:(nch + 1) * T],
                                  start=first, stop=False)
                        first = False
                        ins = P_.matmul(
                            pb[b][:],
                            sb_CimnT[:, nch * D + m * 128:nch * D + (m + 1) * 128],
                            sb_himn[:, nch * T:(nch + 1) * T],
                            start=False, stop=(nch == 1))
                    evm = C.mark(ins, "sp")
                    if m == 7:
                        ev_h_cons[0] = evm
                    C.wait(V, evm)
                    if m == 0:
                        C.wait(V, ev_rly)
                    ins = V.tensor_tensor(out=sb_y[:, m * T:(m + 1) * T],
                                          in0=pb[b][:],
                                          in1=sb_y[:, m * T:(m + 1) * T],
                                          op=ALU.add)
                    ev_yadd[m] = C.mark(ins, "sv")
                    bank_ev[b] = ev_yadd[m]

            def w2_load(m):
                slot = w2_ring.tick()
                C.wait(G, w2_ring.free_ev[slot])
                ev = C.dma(G, sb_w2r[:, slot * DH1:(slot + 1) * DH1],
                           sW2[m, :, :], w2_ring.sems[slot])
                return (slot, ev)

            ev_rly = reload_y(0, ev_allspill)
            ev_rlx = reload_xg(0, None)
            ev_xl = load_x_g(0)
            rhs_stage(0)
            corr_stage(0, ev_rly)

            for tile in range(NT):
                ev_h1 = [None] * 32
                for ng in range(8):
                    bks = [(ng * 4 + nb) % 6 for nb in range(4)]
                    ins = None
                    for kk in range(0, 8, 2):
                        slots = []
                        for k in (kk, kk + 1):
                            slot = w1_ring.tick()
                            C.wait(Q, w1_ring.free_ev[slot])
                            if tile == 0 and ng == 0 and k == 0:
                                C.wait(Q, ev_cast_all[0])
                                C.wait(Q, ev_cast_all[1])
                            slots.append((k, slot,
                                          C.dma(Q, sb_w1r[:, slot * 512:(slot + 1) * 512],
                                                sW1[k * 128:(k + 1) * 128, ng * 512:(ng + 1) * 512],
                                                w1_ring.sems[slot])))
                        for k, slot, eld in slots:
                            C.wait(P_, eld)
                        if kk == 0:
                            for b in bks:
                                C.wait(P_, bank_ev[b])
                        for k, slot, eld in slots:
                            if ng == 0:
                                C.wait(P_, ev_yadd[k])
                            for nb in range(4):
                                ins = P_.matmul(
                                    pb[bks[nb]][:],
                                    sb_w1r[:, slot * 512 + nb * 128:slot * 512 + (nb + 1) * 128],
                                    sb_y[:, k * T:(k + 1) * T],
                                    start=(k == 0), stop=(k == 7))
                            w1_ring.free_ev[slot] = C.mark(ins, "sp")
                    evk = w1_ring.free_ev[(w1_ring.i - 1) % w1_ring.n]
                    for nb in range(4):
                        n = ng * 4 + nb
                        C.wait(S_, evk)
                        if n == 0:
                            C.wait(S_, ev_h1_last[0])
                        ins = S_.activation(out=sb_big[:, n * T:(n + 1) * T],
                                            in_=pb[bks[nb]][:],
                                            func=AF.Gelu_apprx_tanh)
                        ev_h1[n] = C.mark(ins, "ss")
                        bank_ev[bks[nb]] = ev_h1[n]
                ev_w1_yread = w1_ring.free_ev[(w1_ring.i - 1) % w1_ring.n]

                pre = [w2_load(0), w2_load(1)]
                if tile < NT - 1:
                    ev_rly = reload_y(tile + 1, ev_w1_yread)
                    rhs_stage(tile + 1)
                    corr_stage(tile + 1, ev_rly)

                for m in range(8):
                    slot, eld = pre[m]
                    b = 4 + (m % 2)
                    C.wait(P_, bank_ev[b])
                    C.wait(P_, eld)
                    ins = None
                    for n in range(32):
                        if n == 0:
                            C.wait(P_, ev_h1[31])
                        ins = P_.matmul(
                            pb[b][:],
                            sb_w2r[:, slot * DH1 + n * 128:slot * DH1 + (n + 1) * 128],
                            sb_big[:, n * T:(n + 1) * T],
                            start=(n == 0), stop=(n == 31))
                    w2_ring.free_ev[slot] = C.mark(ins, "sp")
                    if m == 7:
                        ev_h1_last[0] = w2_ring.free_ev[slot]
                    if m + 2 < 8:
                        pre.append(w2_load(m + 2))
                    C.wait(V, w2_ring.free_ev[slot])
                    if m == 0:
                        C.wait(V, ev_rlx)
                        C.wait(V, ev_xn_trans[0])
                    V.scalar_tensor_tensor(out=sb_tA[:],
                                           in0=pb[b][:], scalar=sb_sig[:, 0:1],
                                           in1=sb_xnf[:, m * T:(m + 1) * T],
                                           op0=ALU.mult, op1=ALU.add)
                    ins = V.tensor_tensor(out=sb_xn[:, m * T:(m + 1) * T],
                                          in0=sb_tA[:],
                                          in1=sb_gA[:, m * T:(m + 1) * T],
                                          op=ALU.mult)
                    ev_p[m] = C.mark(ins, "sv")
                    bank_ev[b] = ev_p[m]

                if tile < NT - 1:
                    ev_rlx = reload_xg(tile + 1, ev_p[7])

                ev_odma = None
                insB = None
                for tc in range(4):
                    C.wait(P_, pto_ev[0])
                    C.wait(P_, pto_ev[1])
                    for m in range(8):
                        bk = m // 4
                        C.wait(P_, ev_p[m])
                        ins = P_.transpose(pto[bk][:, (m % 4) * 128:(m % 4 + 1) * 128],
                                           sb_xn[:, m * T + tc * 128:m * T + (tc + 1) * 128],
                                           sb_identb[:])
                        if m == 3:
                            insA = ins
                        if m == 7:
                            insB = ins
                    C.mark(insA, "sp")
                    evB = C.mark(insB, "sp")
                    C.wait(V, evB)
                    if tc == 0:
                        C.wait(V, ev_xl)
                    V.tensor_tensor(out=sb_x[:, tc * D:tc * D + 512],
                                    in0=sb_x[:, tc * D:tc * D + 512],
                                    in1=pto[0][:], op=ALU.add)
                    ins = V.tensor_tensor(out=sb_x[:, tc * D + 512:(tc + 1) * D],
                                          in0=sb_x[:, tc * D + 512:(tc + 1) * D],
                                          in1=pto[1][:], op=ALU.add)
                    ev_add = C.mark(ins, "sv")
                    pto_ev[0] = ev_add
                    pto_ev[1] = ev_add
                    C.wait(G, ev_add)
                    ev_odma = C.dma(G, out_o[(tile * 4 + tc) * 128:(tile * 4 + tc + 1) * 128, :],
                                    sb_x[:, tc * D:(tc + 1) * D], "gq")

                ev_x_free[0] = ev_odma
                ev_xn_trans[0] = C.at("sp")
                if tile < NT - 1:
                    ev_xl = load_x_g(tile + 1)

            G.wait_ge(sems["gq"], C.cnt["gq"])

        @block.sync
        def _(sync):
            emit_program(Ctx(nc, "snc", sems))

        @block.gpsimd
        def _(gpsimd):
            emit_program(Ctx(nc, "gps", sems))

        @block.vector
        def _(vector):
            emit_program(Ctx(nc, "vec", sems))

        @block.scalar
        def _(scalar):
            emit_program(Ctx(nc, "scl", sems))

        @block.tensor
        def _(tensor):
            emit_program(Ctx(nc, "pe", sems))

    return nc


_NC_CACHE = [None]


def kernel(L0_attn_x3d, L0_attn_nu_log, L0_attn_theta_log, L0_attn_B_re,
           L0_attn_B_im, L0_attn_C_re, L0_attn_C_im, L0_attn_D_mat, L0_attn_W1,
           L0_attn_W2, L0_attn_res_logit, L0_ffn_w_gate, L0_ffn_w_up,
           L0_ffn_w_down):
    x = np.ascontiguousarray(np.asarray(L0_attn_x3d, np.float32).reshape(2 * 16384, D))
    BreT = np.ascontiguousarray(np.asarray(L0_attn_B_re, np.float32).T)
    BimT = np.ascontiguousarray(np.asarray(L0_attn_B_im, np.float32).T)
    CreT = np.ascontiguousarray(np.asarray(L0_attn_C_re, np.float32).T)
    CimT = np.ascontiguousarray(np.asarray(L0_attn_C_im, np.float32).T)
    DT = np.ascontiguousarray(np.asarray(L0_attn_D_mat, np.float32).T)
    W1 = np.ascontiguousarray(np.asarray(L0_attn_W1, np.float32))
    W2 = np.ascontiguousarray(np.asarray(L0_attn_W2, np.float32))
    WGT = np.ascontiguousarray(np.asarray(L0_ffn_w_gate, np.int32).T)
    WUT = np.ascontiguousarray(np.asarray(L0_ffn_w_up, np.int32).T)
    WDT = np.ascontiguousarray(np.asarray(L0_ffn_w_down, np.int32).T)
    nu = np.ascontiguousarray(np.asarray(L0_attn_nu_log, np.float32).reshape(NST, 1))
    th = np.ascontiguousarray(np.asarray(L0_attn_theta_log, np.float32).reshape(NST, 1))
    rsl = np.ascontiguousarray(
        np.broadcast_to(np.asarray(L0_attn_res_logit, np.float32).reshape(1, 1),
                        (128, 1)))

    if _NC_CACHE[0] is None:
        _NC_CACHE[0] = build_nc()
    nc = _NC_CACHE[0]

    in_maps = []
    for c in range(NCORE):
        msel = np.zeros((8, 3), np.float32)
        for k in range(3):
            j = c - 1 - k
            if j >= 0 and (j // 4) == (c // 4):
                msel[j, k] = 1.0
        msel = np.ascontiguousarray(
            np.broadcast_to(msel.reshape(1, 24), (128, 24)).astype(np.float32))
        in_maps.append(dict(
            x=np.ascontiguousarray(x[c * SEG:(c + 1) * SEG]),
            BreT=BreT, BimT=BimT, CreT=CreT, CimT=CimT, DT=DT,
            W1=W1, W2=W2, WGT=WGT, WUT=WUT, WDT=WDT,
            nu_log=nu, th_log=th, rsl=rsl, msel=msel))
    res = run_bass_kernel_spmd(nc, in_maps, core_ids=list(range(NCORE)))
    out = np.concatenate([np.asarray(r["out"], np.float32) for r in res.results],
                         axis=0)
    return out.reshape(2, 16384, D)


# version 17
# speedup vs baseline: 1.0209x; 1.0051x over previous
import numpy as np
import concourse.bass as bass
import concourse.mybir as mybir
from concourse.bass_utils import run_bass_kernel_spmd

F32 = mybir.dt.float32
BF16 = mybir.dt.bfloat16
I32 = mybir.dt.int32
AF = mybir.ActivationFunctionType
ALU = mybir.AluOpType
AX = mybir.AxisListType
PI = float(np.pi)
INV2PI = 1.0 / (2.0 * PI)

DEBUG = False
NCORE = 8
SEG = 4096
T = 512
NT = SEG // T
W = 128
NCK = T // W
D = 1024
NST = 256
DH1 = 4096
DH2 = 3072


class _Fake:
    def then_inc(self, *a, **k):
        return self


_FAKE = _Fake()


class EngProxy:
    def __init__(self, real, active):
        object.__setattr__(self, "_real", real)
        object.__setattr__(self, "_active", active)
        object.__setattr__(self, "_last", None)

    def __getattr__(self, name):
        if object.__getattribute__(self, "_active"):
            fn = getattr(object.__getattribute__(self, "_real"), name)
            if not callable(fn):
                return fn

            def wrap(*a, **k):
                r = fn(*a, **k)
                object.__setattr__(self, "_last", r)
                return r
            return wrap
        return lambda *a, **k: _FAKE


class Ctx:
    def __init__(self, nc, role, sems):
        self.nc = nc
        self.role = role
        self.sems = sems
        self.cnt = {k: 0 for k in sems}
        self.pe = EngProxy(nc.tensor, role == "pe")
        self.vec = EngProxy(nc.vector, role == "vec")
        self.scl = EngProxy(nc.scalar, role == "scl")
        self.gps = EngProxy(nc.gpsimd, role == "gps")
        self.snc = EngProxy(nc.sync, role == "snc")

    def mark_last(self, proxy, sem, n=1):
        ins = object.__getattribute__(proxy, "_last") \
            if object.__getattribute__(proxy, "_active") else None
        return self.mark(ins, sem, n)

    def mark(self, ins, sem, n=1):
        self.cnt[sem] += n
        if ins is not None and ins is not _FAKE:
            ins.then_inc(self.sems[sem], n)
        return (sem, self.cnt[sem])

    def dma(self, eng, out, in_, sem):
        ins = eng.dma_start(out=out, in_=in_)
        return self.mark(ins, sem, 16)

    def wait(self, eng, ev):
        if ev is None:
            return
        sem, val = ev
        if val > 0:
            eng.wait_ge(self.sems[sem], val)

    def at(self, sem):
        return (sem, self.cnt[sem])


class Ring:

    def __init__(self, nslots, sembase):
        self.n = nslots
        self.free_ev = [None] * nslots
        self.sems = [f"{sembase}{s}" for s in range(nslots)]
        self.i = 0

    def tick(self):
        s = self.i % self.n
        self.i += 1
        return s


def build_nc():
    nc = bass.Bass("TRN2", target_bir_lowering=False, debug=False)

    x_in = nc.dram_tensor("x", [SEG, D], F32, kind="ExternalInput").ap()
    BreT_i = nc.dram_tensor("BreT", [D, NST], F32, kind="ExternalInput").ap()
    BimT_i = nc.dram_tensor("BimT", [D, NST], F32, kind="ExternalInput").ap()
    CreT_i = nc.dram_tensor("CreT", [NST, D], F32, kind="ExternalInput").ap()
    CimT_i = nc.dram_tensor("CimT", [NST, D], F32, kind="ExternalInput").ap()
    DT_i = nc.dram_tensor("DT", [D, D], F32, kind="ExternalInput").ap()
    W1_i = nc.dram_tensor("W1", [D, DH1], F32, kind="ExternalInput").ap()
    W2_i = nc.dram_tensor("W2", [DH1, D], F32, kind="ExternalInput").ap()
    WGT_i = nc.dram_tensor("WGT", [D, DH2], I32, kind="ExternalInput").ap()
    WUT_i = nc.dram_tensor("WUT", [D, DH2], I32, kind="ExternalInput").ap()
    WDT_i = nc.dram_tensor("WDT", [DH2, D], I32, kind="ExternalInput").ap()
    nu_log = nc.dram_tensor("nu_log", [NST, 1], F32, kind="ExternalInput").ap()
    th_log = nc.dram_tensor("th_log", [NST, 1], F32, kind="ExternalInput").ap()
    rsl = nc.dram_tensor("rsl", [128, 1], F32, kind="ExternalInput").ap()
    msel_i = nc.dram_tensor("msel", [128, 24], F32, kind="ExternalInput").ap()
    out_o = nc.dram_tensor("out", [SEG, D], F32, kind="ExternalOutput").ap()

    _skind = "ExternalOutput" if DEBUG else "Internal"
    sW1 = nc.dram_tensor("sW1", [D, DH1], BF16, kind=_skind).ap()
    sGU = nc.dram_tensor("sGU", [D, 8, 768], BF16, kind=_skind).ap()
    sW2 = nc.dram_tensor("sW2", [8, 128, DH1], BF16, kind=_skind).ap()
    sWD = nc.dram_tensor("sWD", [8, 128, DH2], BF16, kind=_skind).ap()
    sBC = nc.dram_tensor("sBC", [D, 512], BF16, kind=_skind).ap()
    sDT = nc.dram_tensor("sDT", [D, D], BF16, kind=_skind).ap()
    sY = nc.dram_tensor("sY", [NT, 128, 8 * T], BF16, kind=_skind).ap()
    sXN = nc.dram_tensor("sXN", [NT, 128, 8 * T], BF16, kind=_skind).ap()
    sG = nc.dram_tensor("sG", [NT, 128, 8 * T], BF16, kind=_skind).ap()
    cc_in = nc.dram_tensor("cc_in", [512], F32).ap()
    cc_out = nc.dram_tensor("cc_out", [NCORE * 512], F32).ap()

    from contextlib import ExitStack
    es = ExitStack()
    with es:
        def sbt(name, shape, dt=F32):
            return es.enter_context(nc.sbuf_tensor(name, shape, dt))

        sb_CreT = sbt("sb_CreT", [128, 2 * D], BF16)
        sb_CimnT = sbt("sb_CimnT", [128, 2 * D], BF16)
        sb_iota = sbt("sb_iota", [128, T])
        sb_iot1 = sbt("sb_iot1", [128, W])
        sb_cosA = sbt("sb_cosA", [128, 2 * T])
        sb_sinA = sbt("sb_sinA", [128, 2 * T])
        sb_rpw = sbt("sb_rpw", [128, 2 * T])
        sb_pw = sbt("sb_pw", [128, 2 * W])
        sb_rb = sbt("sb_rb", [128, 2 * W])
        sb_rho = sbt("sb_rho", [128, 2 * NCK])
        sb_theta = sbt("sb_theta", [128, 2])
        sb_lnr = sbt("sb_lnr", [128, 2])
        sb_r = sbt("sb_r", [128, 2])
        sb_gam = sbt("sb_gam", [128, 2])
        sb_cs1 = sbt("sb_cs1", [128, 4])
        sb_rot = sbt("sb_rot", [128, 4])
        sb_P512 = sbt("sb_P512", [128, 4])
        sb_S0 = sbt("sb_S0", [128, 8])
        sb_sc4 = sbt("sb_sc4", [128, 8])
        sb_sig = sbt("sb_sig", [128, 1])
        sb_identb = sbt("sb_identb", [128, 128], BF16)
        sb_mstat = sbt("sb_mstat", [128, 32])
        sb_nstat = sbt("sb_nstat", [128, 32])
        sb_msel = sbt("sb_msel", [128, 24])
        sb_Hin = sbt("sb_Hin", [128, 4])
        sb_H = sbt("sb_H", [128, 4 * NCK])
        sb_S = sbt("sb_S", [128, 4])
        sb_gath = sbt("sb_gath", [128, 32])
        sb_TK = sbt("sb_TK", [128, 12])
        sb_P = sbt("sb_P", [128, 12])
        sb_acc = sbt("sb_acc", [128, 4])
        sb_acc2 = sbt("sb_acc2", [128, 4])
        sb_x = sbt("sb_x", [128, 4 * D])
        sb_xn = sbt("sb_xn", [128, 4 * D], BF16)
        sb_xnf = sbt("sb_xnf", [128, 8 * T], BF16)
        sb_gre = sbt("sb_gre", [128, T])
        sb_gim = sbt("sb_gim", [128, T])
        sb_bu = sbt("sb_bu", [128, 2 * T])
        sb_scr = sbt("sb_scr", [128, 2 * T])
        sb_mi = sbt("sb_mi", [128, T], I32)
        sb_tA = sbt("sb_tA", [128, T])
        sb_tB = sbt("sb_tB", [128, T])
        sb_hre = sbt("sb_hre", [128, 2 * T], BF16)
        sb_himn = sbt("sb_himn", [128, 2 * T], BF16)
        sb_y = sbt("sb_y", [128, 8 * T], BF16)
        sb_gA = sbt("sb_gA", [128, 8 * T], BF16)
        sb_big = sbt("sb_big", [128, 32 * T], BF16)
        sb_br = sbt("sb_br", [128, 2 * 512], BF16)
        sb_dr = sbt("sb_dr", [128, 2 * D], BF16)
        sb_gur = sbt("sb_gur", [128, 4 * 768], BF16)
        sb_wdr = sbt("sb_wdr", [128, 2 * DH2], BF16)
        sb_w1r = sbt("sb_w1r", [128, 4 * 512], BF16)
        sb_w2r = sbt("sb_w2r", [128, 2 * DH1], BF16)
        sb_stg = sbt("sb_stg", [128, 2 * 1024])
        sb_stgi = sbt("sb_stgi", [128, 2 * 1024], I32)
        sb_stgo = sbt("sb_stgo", [128, 2 * 1024], BF16)

        pb = [es.enter_context(nc.psum_tensor(f"pb{i}", [128, 512], F32))
              for i in range(6)]
        ptx = [es.enter_context(nc.psum_tensor(f"ptx{i}", [128, 512], BF16))
               for i in range(2)]
        pto = ptx

        sems = {}
        for name in (["xq", "pq", "rly", "rlx", "cg", "gq", "sv", "ss", "sg", "sp", "cc",
                      "st0", "st1", "si0", "si1", "so0", "so1",
                      "br0", "br1", "dr0", "dr1",
                      "gu0", "gu1", "gu2", "gu3", "wd0", "wd1",
                      "w10", "w11", "w12", "w13", "w20", "w21"]):
            sems[name] = es.enter_context(nc.semaphore(name))

        block = es.enter_context(nc.Block())

        def emit_program(C: Ctx):
            V, S_, G, P_, Q = C.vec, C.scl, C.gps, C.pe, C.snc

            def VF():
                C.wait(V, C.mark_last(V, "sv"))

            bank_ev = [None] * 6
            ptx_ev = [None] * 2
            pto_ev = ptx_ev

            C.dma(Q, sb_theta[:, 0:1], th_log[0:128, :], "pq")
            C.dma(Q, sb_theta[:, 1:2], th_log[128:256, :], "pq")
            C.dma(Q, sb_lnr[:, 0:1], nu_log[0:128, :], "pq")
            C.dma(Q, sb_lnr[:, 1:2], nu_log[128:256, :], "pq")
            C.dma(Q, sb_sig[:], rsl, "pq")
            ev_params = C.dma(Q, sb_msel[:], msel_i, "pq")

            ev_x_free = [None]

            def load_x(tile):
                C.wait(Q, ev_x_free[0])
                ev = None
                for tc in range(4):
                    ev = C.dma(Q, sb_x[:, tc * D:(tc + 1) * D],
                               x_in[(tile * 4 + tc) * 128:(tile * 4 + tc + 1) * 128, :],
                               "xq")
                return ev

            ev_x0 = load_x(0)

            G.iota(sb_iota[:], pattern=[[1, T]], base=0, channel_multiplier=0,
                   allow_small_or_imprecise_dtypes=True)
            G.iota(sb_iot1[:], pattern=[[1, W]], base=1, channel_multiplier=0,
                   allow_small_or_imprecise_dtypes=True)
            G.memset(sb_identb[:], 0.0)
            G.affine_select(out=sb_identb[:], in_=sb_identb[:],
                            compare_op=ALU.not_equal, fill=1.0, base=0,
                            pattern=[[-1, 128]], channel_multiplier=1)
            G.memset(sb_Hin[:], 0.0)
            ev_gsetup = C.mark_last(G, "sg")

            C.wait(S_, ev_params)
            S_.activation(out=sb_theta[:, 0:2], in_=sb_theta[:, 0:2], func=AF.Exp)
            S_.activation(out=sb_r[:, 0:2], in_=sb_lnr[:, 0:2], func=AF.Exp)
            S_.activation(out=sb_sig[:], in_=sb_sig[:], func=AF.Sigmoid)
            ev = C.mark_last(S_, "ss")
            C.wait(V, ev)
            V.tensor_scalar_mul(sb_lnr[:, 0:2], sb_r[:, 0:2], -1.0)
            ev = C.mark_last(V, "sv")
            C.wait(S_, ev)
            S_.activation(out=sb_r[:, 0:2], in_=sb_lnr[:, 0:2], func=AF.Exp)
            C.wait(S_, ev_gsetup)
            for nch in range(2):
                S_.activation(out=sb_pw[:, nch * W:(nch + 1) * W], in_=sb_iot1[:],
                              func=AF.Exp, scale=sb_lnr[:, nch:nch + 1])
            ev_s2 = C.mark_last(S_, "ss")
            C.wait(V, ev_s2)
            for nch in range(2):
                V.tensor_scalar(out=sb_scr[:, nch * T:(nch + 1) * T], in0=sb_iota[:],
                                scalar1=sb_lnr[:, nch:nch + 1], scalar2=-85.0,
                                op0=ALU.mult, op1=ALU.max)
            ev = C.mark_last(V, "sv")
            C.wait(S_, ev)
            for nch in range(2):
                S_.activation(out=sb_rpw[:, nch * T:(nch + 1) * T],
                              in_=sb_scr[:, nch * T:(nch + 1) * T], func=AF.Exp)
            ev_rpw = C.mark_last(S_, "ss")
            C.wait(V, ev_rpw)
            V.tensor_mul(sb_gam[:, 0:2], sb_r[:, 0:2], sb_r[:, 0:2])
            VF()
            V.tensor_scalar(out=sb_gam[:, 0:2], in0=sb_gam[:, 0:2], scalar1=-1.0,
                            scalar2=1.0, op0=ALU.mult, op1=ALU.add)
            ev = C.mark_last(V, "sv")
            C.wait(S_, ev)
            S_.activation(out=sb_gam[:, 0:2], in_=sb_gam[:, 0:2], func=AF.Sqrt)
            ev_gam = C.mark_last(S_, "ss")
            C.wait(V, ev_gam)
            V.memset(sb_rb[:], 1.0)
            V.memset(sb_rho[:], 1.0)
            V.memset(sb_sc4[:, 4:5], PI / 2.0)
            for nch in range(2):
                V.tensor_scalar_mul(sb_rb[:, nch * W:(nch + 1) * W],
                                    sb_rb[:, nch * W:(nch + 1) * W],
                                    sb_r[:, nch:nch + 1])
                V.tensor_scalar_mul(sb_rho[:, nch * NCK:(nch + 1) * NCK],
                                    sb_rho[:, nch * NCK:(nch + 1) * NCK],
                                    sb_pw[:, nch * W + W - 1:nch * W + W])
            ev_bcast = C.mark_last(V, "sv")

            def trig_cols(val, dst):
                V.tensor_scalar_mul(sb_sc4[:, 0:2], sb_theta[:, 0:2], val * INV2PI)
                VF()
                V.tensor_copy(out=sb_mi[:, 0:2], in_=sb_sc4[:, 0:2])
                VF()
                V.tensor_copy(out=sb_sc4[:, 2:4], in_=sb_mi[:, 0:2])
                VF()
                V.tensor_sub(sb_sc4[:, 0:2], sb_sc4[:, 0:2], sb_sc4[:, 2:4])
                VF()
                V.tensor_scalar_mul(sb_sc4[:, 0:2], sb_sc4[:, 0:2], PI / 2.0)
                ev = C.mark_last(V, "sv")
                C.wait(S_, ev)
                S_.activation(out=sb_sc4[:, 2:4], in_=sb_sc4[:, 0:2], func=AF.Sin)
                S_.activation(out=sb_sc4[:, 6:8], in_=sb_sc4[:, 0:2], func=AF.Sin,
                              scale=-1.0, bias=sb_sc4[:, 4:5])
                ev = C.mark_last(S_, "ss")
                C.wait(V, ev)
                V.tensor_mul(sb_sc4[:, 0:2], sb_sc4[:, 2:4], sb_sc4[:, 6:8])
                VF()
                V.tensor_scalar_mul(sb_sc4[:, 0:2], sb_sc4[:, 0:2], 2.0)
                V.tensor_mul(dst[:, 0:2], sb_sc4[:, 2:4], sb_sc4[:, 2:4])
                VF()
                V.tensor_scalar(out=dst[:, 0:2], in0=dst[:, 0:2], scalar1=-2.0,
                                scalar2=1.0, op0=ALU.mult, op1=ALU.add)
                VF()
                V.tensor_mul(dst[:, 2:4], sb_sc4[:, 0:2], dst[:, 0:2])
                VF()
                V.tensor_scalar_mul(dst[:, 2:4], dst[:, 2:4], 2.0)
                V.tensor_mul(sb_sc4[:, 2:4], sb_sc4[:, 0:2], sb_sc4[:, 0:2])
                VF()
                V.tensor_scalar(out=dst[:, 0:2], in0=sb_sc4[:, 2:4], scalar1=-2.0,
                                scalar2=1.0, op0=ALU.mult, op1=ALU.add)
                return C.mark_last(V, "sv")

            trig_cols(1.0, sb_cs1)
            ev_rot = trig_cols(512.0, sb_rot)

            ev_tables = [None]

            def build_tables_tile0():
                for nch in range(2):
                    cs = sb_cosA[:, nch * T:(nch + 1) * T]
                    sn = sb_sinA[:, nch * T:(nch + 1) * T]
                    V.tensor_scalar(out=sb_tA[:], in0=sb_iota[:],
                                    scalar1=sb_theta[:, nch:nch + 1], scalar2=INV2PI,
                                    op0=ALU.mult, op1=ALU.mult)
                    V.tensor_copy(out=sb_mi[:], in_=sb_tA[:])
                    V.tensor_copy(out=sb_tB[:], in_=sb_mi[:])
                    V.tensor_sub(sb_tA[:], sb_tA[:], sb_tB[:])
                    V.tensor_scalar_mul(sb_tA[:], sb_tA[:], PI / 2.0)
                    ev = C.mark_last(V, "sv")
                    C.wait(S_, ev)
                    S_.activation(out=sb_tB[:], in_=sb_tA[:], func=AF.Sin)
                    S_.activation(out=sb_gre[:], in_=sb_tA[:], func=AF.Sin,
                                  scale=-1.0, bias=sb_sc4[:, 4:5])
                    ev = C.mark_last(S_, "ss")
                    C.wait(V, ev)
                    V.tensor_mul(sb_tA[:], sb_tB[:], sb_gre[:])
                    V.tensor_scalar_mul(sb_tA[:], sb_tA[:], 2.0)
                    V.tensor_mul(cs, sb_tB[:], sb_tB[:])
                    V.tensor_scalar(out=cs, in0=cs, scalar1=-2.0, scalar2=1.0,
                                    op0=ALU.mult, op1=ALU.add)
                    V.tensor_mul(sn, sb_tA[:], cs)
                    V.tensor_scalar_mul(sn, sn, 2.0)
                    V.tensor_mul(sb_tB[:], sb_tA[:], sb_tA[:])
                    V.tensor_scalar(out=cs, in0=sb_tB[:], scalar1=-2.0, scalar2=1.0,
                                    op0=ALU.mult, op1=ALU.add)
                ev_tables[0] = C.mark_last(V, "sv")

            def rotate_tables():
                C.wait(V, ev_rot)
                for nch in range(2):
                    oc = sb_cosA[:, nch * T:(nch + 1) * T]
                    os_ = sb_sinA[:, nch * T:(nch + 1) * T]
                    c5 = sb_rot[:, nch:nch + 1]
                    s5 = sb_rot[:, 2 + nch:3 + nch]
                    V.tensor_scalar_mul(sb_tA[:], oc, c5)
                    V.tensor_scalar_mul(sb_tB[:], os_, s5)
                    V.tensor_sub(sb_gre[:], sb_tA[:], sb_tB[:])
                    V.tensor_scalar_mul(sb_tA[:], os_, c5)
                    V.tensor_scalar_mul(sb_tB[:], oc, s5)
                    V.tensor_add(sb_gim[:], sb_tA[:], sb_tB[:])
                    V.tensor_copy(out=oc, in_=sb_gre[:])
                    V.tensor_copy(out=os_, in_=sb_gim[:])
                ev_tables[0] = C.mark_last(V, "sv")

            cast_state = {
                0: {"stg": None, "sti": None, "sto": None},
                1: {"stg": None, "sti": None, "sto": None},
            }
            cast_ix = [0]

            def cast_job(src, dst, int_src=False, dst_sb=None, dst_view=None):
                h = cast_ix[0] % 2
                cast_ix[0] += 1
                E = S_ if h == 0 else V
                esem = "ss" if h == 0 else "sv"
                st = cast_state[h]
                ncols = src.shape[1]
                if int_src:
                    stgb = sb_stgi[:, h * 1024:h * 1024 + ncols]
                    lsem = f"si{h}"
                    C.wait(Q, st["sti"])
                else:
                    stgb = sb_stg[:, h * 1024:h * 1024 + ncols]
                    lsem = f"st{h}"
                    C.wait(Q, st["stg"])
                evl = C.dma(Q, stgb, src, lsem)
                C.wait(E, evl)
                if dst_sb is not None:
                    ins = E.tensor_copy(out=dst_sb, in_=stgb) if h == 1 else \
                        E.copy(out=dst_sb, in_=stgb)
                    evc = C.mark(ins, esem)
                    if int_src:
                        st["sti"] = evc
                    else:
                        st["stg"] = evc
                    return evc
                C.wait(E, st["sto"])
                ob = sb_stgo[:, h * 1024:h * 1024 + ncols]
                ins = E.tensor_copy(out=ob, in_=stgb) if h == 1 else \
                    E.copy(out=ob, in_=stgb)
                evc = C.mark(ins, esem)
                if int_src:
                    st["sti"] = evc
                else:
                    st["stg"] = evc
                C.wait(G, evc)
                if dst_view is not None:
                    evs = C.dma(G, dst, dst_view(ob), f"so{h}")
                else:
                    evs = C.dma(G, dst, ob, f"so{h}")
                st["sto"] = evs
                return evs

            ev_Cres = None
            for nch in range(2):
                cast_job(CreT_i[nch * 128:(nch + 1) * 128, :],
                         None, dst_sb=sb_CreT[:, nch * D:(nch + 1) * D])
                ev_Cres = cast_job(CimT_i[nch * 128:(nch + 1) * 128, :],
                                   None, dst_sb=sb_CimnT[:, nch * D:(nch + 1) * D])
            ev_Cres2 = C.at("ss")
            ev_Cres3 = C.at("sv")

            for dc in range(8):
                cast_job(BreT_i[dc * 128:(dc + 1) * 128, :],
                         sBC[dc * 128:(dc + 1) * 128, 0:256])
                cast_job(BimT_i[dc * 128:(dc + 1) * 128, :],
                         sBC[dc * 128:(dc + 1) * 128, 256:512])
            ev_B = (C.at("so0"), C.at("so1"))

            cast_GU = {}

            def emit_gu_casts(jg):
                for k in range(8):
                    for which, srcm in (("g", WGT_i), ("u", WUT_i)):
                        hoff = 0 if which == "g" else 384
                        src = srcm[k * 128:(k + 1) * 128, jg * 768:(jg + 1) * 768]
                        dst = sGU[k * 128:(k + 1) * 128, 2 * jg:2 * jg + 2, hoff:hoff + 384]
                        ev = cast_job(src, dst, int_src=True,
                                      dst_view=lambda ob: ob.rearrange(
                                          "p (g c) -> p g c", c=384))
                        cast_GU[(k, jg, which)] = ev

            def emit_wd_casts():
                for n in range(24):
                    cast_job(WDT_i[n * 128:(n + 1) * 128, :],
                             sWD[:, :, n * 128:(n + 1) * 128].rearrange(
                                 "m p c -> p m c"),
                             int_src=True,
                             dst_view=lambda ob: ob.rearrange(
                                 "p (m c) -> p m c", m=8))
                return (C.at("so0"), C.at("so1"))

            def emit_d_casts():
                for dc in range(8):
                    cast_job(DT_i[dc * 128:(dc + 1) * 128, :],
                             sDT[dc * 128:(dc + 1) * 128, :])
                return (C.at("so0"), C.at("so1"))

            w12_jobs = []
            for rc in range(8):
                for cp in range(4):
                    w12_jobs.append((W1_i[rc * 128:(rc + 1) * 128,
                                          cp * 1024:(cp + 1) * 1024],
                                     sW1[rc * 128:(rc + 1) * 128,
                                         cp * 1024:(cp + 1) * 1024], False))
            for n in range(32):
                w12_jobs.append((W2_i[n * 128:(n + 1) * 128, :],
                                 sW2[:, :, n * 128:(n + 1) * 128].rearrange(
                                     "m p c -> p m c"), True))
            w12_li = [0]
            w12_ci = [0]
            w12_ld = {}
            w12_stg = {0: None, 1: None}
            w12_sto = {0: None, 1: None}

            def w12_load(n):
                for _ in range(n):
                    j = w12_li[0]
                    if j >= len(w12_jobs):
                        return
                    w12_li[0] = j + 1
                    h = j % 2
                    if w12_stg[h] is None:
                        w12_stg[h] = cast_state[h]["stg"]
                    C.wait(Q, w12_stg[h])
                    src = w12_jobs[j][0]
                    w12_ld[j] = C.dma(Q, sb_stg[:, h * 1024:(h + 1) * 1024],
                                      src, f"st{h}")

            def w12_copy_store(n):
                for _ in range(n):
                    j = w12_ci[0]
                    if j >= min(w12_li[0], len(w12_jobs)):
                        return
                    w12_ci[0] = j + 1
                    h = j % 2
                    if w12_sto[h] is None:
                        w12_sto[h] = cast_state[h]["sto"]
                    C.wait(S_, w12_ld[j])
                    C.wait(S_, w12_sto[h])
                    ins = S_.copy(out=sb_stgo[:, h * 1024:(h + 1) * 1024],
                                  in_=sb_stg[:, h * 1024:(h + 1) * 1024])
                    evc = C.mark(ins, "ss")
                    w12_stg[h] = evc
                    C.wait(Q, evc)
                    src, dst, rearr = w12_jobs[j]
                    ob = sb_stgo[:, h * 1024:(h + 1) * 1024]
                    if rearr:
                        ob = ob.rearrange("p (m c) -> p m c", m=8)
                    w12_sto[h] = C.dma(Q, dst, ob, f"so{h}")

            ev_xn_tile = [None]

            def ln_tile(tile, ev_xload):
                C.wait(V, ev_xload)
                b4 = slice(tile * 4, tile * 4 + 4)
                for tc in range(4):
                    xa = sb_x[:, tc * D:(tc + 1) * D]
                    sc = slice(tile * 4 + tc, tile * 4 + tc + 1)
                    V.tensor_reduce(sb_mstat[:, sc], xa, axis=AX.X, op=ALU.add)
                VF()
                V.tensor_scalar_mul(sb_mstat[:, b4], sb_mstat[:, b4], 1.0 / D)
                VF()
                for tc in range(4):
                    xa = sb_x[:, tc * D:(tc + 1) * D]
                    sc = slice(tile * 4 + tc, tile * 4 + tc + 1)
                    V.scalar_tensor_tensor(out=sb_scr[:, 0:D], in0=xa,
                                           scalar=sb_mstat[:, sc], in1=xa,
                                           op0=ALU.subtract, op1=ALU.mult,
                                           accum_out=sb_nstat[:, sc])
                VF()
                V.tensor_scalar(out=sb_nstat[:, b4], in0=sb_nstat[:, b4],
                                scalar1=1.0 / D, scalar2=1e-5,
                                op0=ALU.mult, op1=ALU.add)
                ev = C.mark_last(V, "sv")
                C.wait(S_, ev)
                S_.activation(out=sb_nstat[:, b4], in_=sb_nstat[:, b4],
                              func=AF.Sqrt)
                ev = C.mark_last(S_, "ss")
                C.wait(V, ev)
                V.reciprocal(out=sb_nstat[:, b4], in_=sb_nstat[:, b4])
                VF()
                for tc in range(4):
                    xa = sb_x[:, tc * D:(tc + 1) * D]
                    sc = slice(tile * 4 + tc, tile * 4 + tc + 1)
                    V.tensor_scalar(out=sb_xn[:, tc * D:(tc + 1) * D], in0=xa,
                                    scalar1=sb_mstat[:, sc], scalar2=sb_nstat[:, sc],
                                    op0=ALU.subtract, op1=ALU.mult)
                ev_xn_tile[0] = C.mark_last(V, "sv")

            ev_xnf = [None]
            ev_xnf_free = [None, None]

            def transpose_xn():
                C.wait(P_, ev_xn_tile[0])
                for dc in range(8):
                    b = dc % 2
                    C.wait(P_, ptx_ev[b])
                    ins = None
                    for tc in range(4):
                        ins = P_.transpose(ptx[b][:, tc * 128:(tc + 1) * 128],
                                           sb_xn[:, tc * D + dc * 128:tc * D + (dc + 1) * 128],
                                           sb_identb[:])
                    ev = C.mark(ins, "sp")
                    C.wait(S_, ev)
                    if dc < 2:
                        C.wait(S_, ev_xnf_free[0])
                        C.wait(S_, ev_xnf_free[1])
                    ins = S_.copy(out=sb_xnf[:, dc * T:(dc + 1) * T], in_=ptx[b][:])
                    ptx_ev[b] = C.mark(ins, "ss")
                ev_xnf[0] = ptx_ev[1]

            br_ring = Ring(2, "br")

            def bu_mms(tile):
                for b in range(4):
                    C.wait(P_, bank_ev[b])
                ins = None
                for dc in range(8):
                    slot = br_ring.tick()
                    C.wait(Q, br_ring.free_ev[slot])
                    if tile == 0 and dc < 2:
                        C.wait(Q, ev_B[0])
                        C.wait(Q, ev_B[1])
                    eld = C.dma(Q, sb_br[:, slot * 512:(slot + 1) * 512],
                                sBC[dc * 128:(dc + 1) * 128, :],
                                br_ring.sems[slot])
                    C.wait(P_, eld)
                    if dc == 0:
                        C.wait(P_, ev_xnf[0])
                    rhs = sb_xnf[:, dc * T:(dc + 1) * T]
                    for nch in range(2):
                        P_.matmul(pb[nch][:],
                                  sb_br[:, slot * 512 + nch * 128:slot * 512 + (nch + 1) * 128],
                                  rhs, start=(dc == 0), stop=(dc == 7))
                        ins = P_.matmul(
                            pb[2 + nch][:],
                            sb_br[:, slot * 512 + 256 + nch * 128:slot * 512 + 256 + (nch + 1) * 128],
                            rhs, start=(dc == 0), stop=(dc == 7))
                    br_ring.free_ev[slot] = C.mark(ins, "sp")
                return br_ring.free_ev[(br_ring.i - 1) % br_ring.n]

            ev_h_cons = [None]

            def tile_scan_stream(tile, ev_bu):
                C.wait(V, ev_bu)
                C.wait(V, ev_tables[0])
                C.wait(V, ev_h_cons[0])
                V.tensor_copy(out=sb_bu[:, 0:T], in_=pb[1][:])
                ins = V.tensor_copy(out=sb_bu[:, T:2 * T], in_=pb[3][:])
                ev_bu1 = C.mark(ins, "sv")
                bank_ev[1] = ev_bu1
                bank_ev[3] = ev_bu1
                for nch in range(2):
                    pre_b = pb[nch][:] if nch == 0 else sb_bu[:, 0:T]
                    pim_b = pb[2 + nch][:] if nch == 0 else sb_bu[:, T:2 * T]
                    cs = sb_cosA[:, nch * T:(nch + 1) * T]
                    sn = sb_sinA[:, nch * T:(nch + 1) * T]
                    gm = sb_gam[:, nch:nch + 1]
                    V.tensor_tensor(out=sb_tA[:], in0=cs, in1=pre_b, op=ALU.mult)
                    V.tensor_tensor(out=sb_tB[:], in0=sn, in1=pim_b, op=ALU.mult)
                    V.tensor_add(sb_tA[:], sb_tA[:], sb_tB[:])
                    V.tensor_scalar_mul(sb_gre[:], sb_tA[:], gm)
                    V.tensor_tensor(out=sb_tA[:], in0=cs, in1=pim_b, op=ALU.mult)
                    V.tensor_tensor(out=sb_tB[:], in0=sn, in1=pre_b, op=ALU.mult)
                    V.tensor_sub(sb_tA[:], sb_tA[:], sb_tB[:])
                    ins = V.tensor_scalar_mul(sb_gim[:], sb_tA[:], gm)
                    ev_mod = C.mark(ins, "sv")
                    if nch == 0:
                        bank_ev[0] = ev_mod
                        bank_ev[2] = ev_mod
                    ins = None
                    for comp in range(2):
                        gbuf = sb_gre if comp == 0 else sb_gim
                        ubuf = sb_scr[:, comp * T:(comp + 1) * T]
                        for c in range(NCK):
                            ins = V.tensor_tensor_scan(
                                out=ubuf[:, c * W:(c + 1) * W],
                                data0=sb_rb[:, nch * W:(nch + 1) * W],
                                data1=gbuf[:, c * W:(c + 1) * W],
                                initial=0.0, op0=ALU.mult, op1=ALU.add)
                    ev = C.mark(ins, "sv")
                    C.wait(V, ev)
                    for comp in range(2):
                        ubuf = sb_scr[:, comp * T:(comp + 1) * T]
                        hcol = sb_H[:, (nch * 2 + comp) * NCK:(nch * 2 + comp + 1) * NCK]
                        ins = V.tensor_tensor_scan(
                            out=hcol,
                            data0=sb_rho[:, nch * NCK:(nch + 1) * NCK],
                            data1=ubuf[:, W - 1:T:W],
                            initial=sb_Hin[:, nch * 2 + comp:nch * 2 + comp + 1],
                            op0=ALU.mult, op1=ALU.add)
                    ev = C.mark(ins, "sv")
                    C.wait(V, ev)
                    for comp in range(2):
                        ubuf = sb_scr[:, comp * T:(comp + 1) * T]
                        hcol = sb_H[:, (nch * 2 + comp) * NCK:(nch * 2 + comp + 1) * NCK]
                        for c in range(NCK):
                            scal = (sb_Hin[:, nch * 2 + comp:nch * 2 + comp + 1]
                                    if c == 0 else hcol[:, c - 1:c])
                            V.scalar_tensor_tensor(
                                out=ubuf[:, c * W:(c + 1) * W],
                                in0=sb_pw[:, nch * W:(nch + 1) * W],
                                scalar=scal,
                                in1=ubuf[:, c * W:(c + 1) * W],
                                op0=ALU.mult, op1=ALU.add)
                    ure = sb_scr[:, 0:T]
                    uim = sb_scr[:, T:2 * T]
                    V.tensor_tensor(out=sb_tA[:], in0=cs, in1=ure, op=ALU.mult)
                    V.tensor_tensor(out=sb_tB[:], in0=sn, in1=uim, op=ALU.mult)
                    V.tensor_sub(sb_hre[:, nch * T:(nch + 1) * T], sb_tA[:], sb_tB[:])
                    V.tensor_tensor(out=sb_tA[:], in0=sn, in1=ure, op=ALU.mult)
                    V.tensor_tensor(out=sb_tB[:], in0=cs, in1=uim, op=ALU.mult)
                    V.tensor_add(sb_tA[:], sb_tA[:], sb_tB[:])
                    V.tensor_scalar_mul(sb_himn[:, nch * T:(nch + 1) * T],
                                        sb_tA[:], -1.0)
                    ins = None
                    for comp in range(2):
                        hcol = sb_H[:, (nch * 2 + comp) * NCK:(nch * 2 + comp + 1) * NCK]
                        ins = V.tensor_copy(
                            out=sb_Hin[:, nch * 2 + comp:nch * 2 + comp + 1],
                            in_=hcol[:, NCK - 1:NCK])
                    ev = C.mark(ins, "sv")
                    C.wait(V, ev)
                return C.at("sv")

            gu_ring = Ring(4, "gu")
            wd_ring = Ring(2, "wd")
            dr_ring = Ring(2, "dr")
            w1_ring = Ring(4, "w1")
            w2_ring = Ring(2, "w2")

            ev_hf = [None] * 24
            ev_g = [None] * 8
            ev_gA_free = [None]
            ev_y_free = [None]
            ev_WD = [None]
            ev_D = [None]

            def gu_stage(tile):
                for ng in range(8):
                    if tile == 0:
                        emit_gu_casts(ng // 2) if ng % 2 == 0 else None
                    insg = insu = None
                    for kk in range(0, 8, 2):
                        slots = []
                        for k in (kk, kk + 1):
                            slot = gu_ring.tick()
                            C.wait(G, gu_ring.free_ev[slot])
                            if tile == 0:
                                C.wait(G, cast_GU[(k, ng // 2, "g")])
                                C.wait(G, cast_GU[(k, ng // 2, "u")])
                            slots.append((k, slot,
                                          C.dma(G, sb_gur[:, slot * 768:(slot + 1) * 768],
                                                sGU[k * 128:(k + 1) * 128, ng, :],
                                                gu_ring.sems[slot])))
                        for k, slot, eld in slots:
                            C.wait(P_, eld)
                        if kk == 0:
                            for nb in range(6):
                                C.wait(P_, bank_ev[nb])
                        for k, slot, eld in slots:
                            for nb in range(3):
                                insg = P_.matmul(
                                    pb[nb][:],
                                    sb_gur[:, slot * 768 + nb * 128:slot * 768 + (nb + 1) * 128],
                                    sb_xnf[:, k * T:(k + 1) * T],
                                    start=(k == 0), stop=(k == 7))
                                insu = P_.matmul(
                                    pb[3 + nb][:],
                                    sb_gur[:, slot * 768 + 384 + nb * 128:slot * 768 + 384 + (nb + 1) * 128],
                                    sb_xnf[:, k * T:(k + 1) * T],
                                    start=(k == 0), stop=(k == 7))
                            gu_ring.free_ev[slot] = C.mark(insu, "sp")
                    evk = gu_ring.free_ev[(gu_ring.i - 1) % gu_ring.n]
                    for nb in range(3):
                        n = ng * 3 + nb
                        C.wait(S_, evk)
                        ins = S_.activation(out=sb_big[:, n * T:(n + 1) * T],
                                            in_=pb[nb][:], func=AF.Silu,
                                            scale=1.0 / 127.0)
                        evsl = C.mark(ins, "ss")
                        bank_ev[nb] = evsl
                        C.wait(V, evsl)
                        ins = V.tensor_tensor(out=sb_big[:, n * T:(n + 1) * T],
                                              in0=sb_big[:, n * T:(n + 1) * T],
                                              in1=pb[3 + nb][:], op=ALU.mult)
                        ev_hf[n] = C.mark(ins, "sv")
                        bank_ev[3 + nb] = ev_hf[n]
                    if tile >= 1:
                        w12_copy_store(1)
                        w12_load(1)

            def wd_stage(tile):
                for m in range(8):
                    b = m % 2
                    C.wait(P_, bank_ev[b])
                    slot = wd_ring.tick()
                    C.wait(G, wd_ring.free_ev[slot])
                    if tile == 0 and m < 2:
                        C.wait(G, ev_WD[0][0])
                        C.wait(G, ev_WD[0][1])
                    eld = C.dma(G, sb_wdr[:, slot * DH2:(slot + 1) * DH2],
                                sWD[m, :, :], wd_ring.sems[slot])
                    C.wait(P_, eld)
                    ins = None
                    for n in range(24):
                        if n == 0:
                            C.wait(P_, ev_hf[23])
                        ins = P_.matmul(
                            pb[b][:],
                            sb_wdr[:, slot * DH2 + n * 128:slot * DH2 + (n + 1) * 128],
                            sb_big[:, n * T:(n + 1) * T],
                            start=(n == 0), stop=(n == 23))
                    wd_ring.free_ev[slot] = C.mark(ins, "sp")
                    C.wait(S_, wd_ring.free_ev[slot])
                    if m == 0:
                        C.wait(S_, ev_gA_free[0])
                    ins = S_.activation(out=sb_gA[:, m * T:(m + 1) * T],
                                        in_=pb[b][:], func=AF.Silu,
                                        scale=1.0 / (127.0 * 127.0))
                    ev_g[m] = C.mark(ins, "ss")
                    bank_ev[b] = ev_g[m]

            ev_y = [None] * 8

            def y_stage(tile, ev_h):
                for half in range(2):
                    ms = range(4 * half, 4 * half + 4)
                    for m in ms:
                        b = 2 + (m % 4)
                        C.wait(P_, bank_ev[b])
                        if m == 0:
                            C.wait(P_, ev_h)
                            if tile == 0:
                                C.wait(P_, ev_Cres2)
                                C.wait(P_, ev_Cres3)
                        first = True
                        for nch in range(2):
                            P_.matmul(pb[b][:],
                                      sb_CreT[:, nch * D + m * 128:nch * D + (m + 1) * 128],
                                      sb_hre[:, nch * T:(nch + 1) * T],
                                      start=first, stop=False)
                            first = False
                            P_.matmul(pb[b][:],
                                      sb_CimnT[:, nch * D + m * 128:nch * D + (m + 1) * 128],
                                      sb_himn[:, nch * T:(nch + 1) * T],
                                      start=False, stop=False)
                    for dc in range(8):
                        slot = dr_ring.tick()
                        C.wait(Q, dr_ring.free_ev[slot])
                        if tile == 0 and half == 0 and dc < 2:
                            C.wait(Q, ev_D[0][0])
                            C.wait(Q, ev_D[0][1])
                        eld = C.dma(Q, sb_dr[:, slot * D:(slot + 1) * D],
                                    sDT[dc * 128:(dc + 1) * 128, :],
                                    dr_ring.sems[slot])
                        C.wait(P_, eld)
                        ins = None
                        for m in ms:
                            b = 2 + (m % 4)
                            ins = P_.matmul(pb[b][:],
                                            sb_dr[:, slot * D + m * 128:slot * D + (m + 1) * 128],
                                            sb_xnf[:, dc * T:(dc + 1) * T],
                                            start=False, stop=(dc == 7))
                        dr_ring.free_ev[slot] = C.mark(ins, "sp")
                    ev_hc = dr_ring.free_ev[(dr_ring.i - 1) % dr_ring.n]
                    ev_h_cons[0] = ev_hc
                    for m in ms:
                        b = 2 + (m % 4)
                        C.wait(S_, ev_hc)
                        if m == 0:
                            C.wait(S_, ev_y_free[0])
                        ins = S_.copy(out=sb_y[:, m * T:(m + 1) * T], in_=pb[b][:])
                        ev_y[m] = C.mark(ins, "ss")
                        bank_ev[b] = ev_y[m]

            ev_sp_y = [None]
            ev_sp_xn = [None]
            ev_sp_g = [None]
            ln_tile(0, ev_x0)
            for tile in range(NT):
                if tile == 1:
                    w12_load(2)
                if tile == 0:
                    build_tables_tile0()
                else:
                    rotate_tables()
                ev_x_free[0] = ev_xn_tile[0]
                transpose_xn()
                ev_bu = bu_mms(tile)
                if tile < NT - 1:
                    ev_xl = load_x(tile + 1)
                ev_h = tile_scan_stream(tile, ev_bu)
                if tile == NT - 1:
                    for nch in range(2):
                        c511 = sb_cosA[:, nch * T + T - 1:nch * T + T]
                        s511 = sb_sinA[:, nch * T + T - 1:nch * T + T]
                        hre = sb_Hin[:, nch * 2 + 0:nch * 2 + 1]
                        him = sb_Hin[:, nch * 2 + 1:nch * 2 + 2]
                        V.tensor_tensor(out=sb_acc[:, 0:1], in0=c511, in1=hre,
                                        op=ALU.mult)
                        V.tensor_tensor(out=sb_acc[:, 1:2], in0=s511, in1=him,
                                        op=ALU.mult)
                        VF()
                        V.tensor_sub(sb_S[:, nch:nch + 1],
                                     sb_acc[:, 0:1], sb_acc[:, 1:2])
                        V.tensor_tensor(out=sb_acc[:, 0:1], in0=s511, in1=hre,
                                        op=ALU.mult)
                        V.tensor_tensor(out=sb_acc[:, 1:2], in0=c511, in1=him,
                                        op=ALU.mult)
                        VF()
                        V.tensor_add(sb_S[:, 2 + nch:3 + nch],
                                     sb_acc[:, 0:1], sb_acc[:, 1:2])
                    VF()
                    ev_S = C.at("sv")
                    C.wait(G, ev_S)
                    C.dma(G, cc_in.rearrange("(p c) -> p c", c=4), sb_S[:], "gq")
                    C.wait(G, C.at("gq"))
                    ins = G.collective_compute(
                        "AllGather", ALU.bypass,
                        replica_groups=[list(range(NCORE))],
                        ins=[cc_in], outs=[cc_out])
                    C.mark(ins, "cc")
                gu_stage(tile)
                if tile < NT - 1:
                    ln_tile(tile + 1, ev_xl)
                C.wait(G, ev_xnf[0])
                ev_sp_xn[0] = C.dma(G, sXN[tile], sb_xnf[:], "gq")
                if tile == 0:
                    ev_WD[0] = emit_wd_casts()
                wd_stage(tile)
                if tile >= 1:
                    w12_copy_store(1)
                    w12_load(1)
                C.wait(G, ev_g[7])
                ev_sp_g[0] = C.dma(G, sG[tile], sb_gA[:], "gq")
                ev_gA_free[0] = ev_sp_g[0]
                if tile == 0:
                    ev_D[0] = emit_d_casts()
                y_stage(tile, ev_h)
                if tile >= 1:
                    w12_copy_store(1)
                    w12_load(1)
                C.wait(G, ev_y[7])
                ev_sp_y[0] = C.dma(G, sY[tile], sb_y[:], "gq")
                ev_y_free[0] = ev_sp_y[0]
                ev_xnf_free[0] = ev_h_cons[0]
                ev_xnf_free[1] = ev_sp_xn[0]
            for _ in range(len(w12_jobs)):
                w12_copy_store(1)
                w12_load(1)
            ev_cast_all = (C.at("so0"), C.at("so1"))

            C.wait(G, C.at("cc"))
            ev_gath = C.dma(G, sb_gath[:].rearrange("p (j c) -> p j c", j=NCORE),
                            cc_out.rearrange("(j p c) -> p j c", j=NCORE, c=4), "cg")

            build_tables_tile0()
            V.tensor_scalar(out=sb_sc4[:, 0:2], in0=sb_lnr[:, 0:2], scalar1=512.0,
                            scalar2=-85.0, op0=ALU.mult, op1=ALU.max)
            ev = C.mark_last(V, "sv")
            C.wait(S_, ev)
            S_.activation(out=sb_P512[:, 0:2], in_=sb_sc4[:, 0:2], func=AF.Exp)
            ev = C.mark_last(S_, "ss")
            C.wait(V, ev)
            C.wait(V, ev_rot)
            V.tensor_mul(sb_P512[:, 2:4], sb_P512[:, 0:2], sb_rot[:, 2:4])
            VF()
            V.tensor_mul(sb_P512[:, 0:2], sb_P512[:, 0:2], sb_rot[:, 0:2])
            VF()
            ev_tabB = ev_tables[0]

            C.wait(V, ev_gath)
            for k in range(3):
                V.memset(sb_acc[:], 0.0)
                VF()
                cur, nxt = sb_acc, sb_acc2
                for j in range(NCORE):
                    V.scalar_tensor_tensor(out=nxt[:], in0=sb_gath[:, 4 * j:4 * j + 4],
                                           scalar=sb_msel[:, j * 3 + k:j * 3 + k + 1],
                                           in1=cur[:], op0=ALU.mult, op1=ALU.add)
                    VF()
                    cur, nxt = nxt, cur
                V.tensor_copy(out=sb_TK[:, 4 * k:4 * k + 4], in_=cur[:])
                VF()
            V.tensor_scalar(out=sb_sc4[:, 0:2], in0=sb_lnr[:, 0:2], scalar1=4096.0,
                            scalar2=-85.0, op0=ALU.mult, op1=ALU.max)
            VF()
            ev = C.mark_last(V, "sv")
            C.wait(S_, ev)
            S_.activation(out=sb_P[:, 0:2], in_=sb_sc4[:, 0:2], func=AF.Exp)
            ev = C.mark_last(S_, "ss")
            C.wait(V, ev)
            trig_cols(4096.0, sb_acc2)
            C.wait(V, ev)
            V.tensor_mul(sb_P[:, 2:4], sb_P[:, 0:2], sb_acc2[:, 2:4])
            V.tensor_mul(sb_P[:, 0:2], sb_P[:, 0:2], sb_acc2[:, 0:2])
            VF()

            def cmul4(dst0, a0, b0):
                V.tensor_mul(sb_acc[:, 0:2], a0[:, 0:2], b0[:, 0:2])
                V.tensor_mul(sb_acc[:, 2:4], a0[:, 2:4], b0[:, 2:4])
                VF()
                V.tensor_sub(dst0[:, 0:2], sb_acc[:, 0:2], sb_acc[:, 2:4])
                V.tensor_mul(sb_acc[:, 0:2], a0[:, 0:2], b0[:, 2:4])
                V.tensor_mul(sb_acc[:, 2:4], a0[:, 2:4], b0[:, 0:2])
                VF()
                V.tensor_add(dst0[:, 2:4], sb_acc[:, 0:2], sb_acc[:, 2:4])
                VF()

            cmul4(sb_P[:, 4:8], sb_P[:, 0:4], sb_P[:, 0:4])
            cmul4(sb_acc2[:], sb_P[:, 0:4], sb_TK[:, 4:8])
            V.tensor_add(sb_TK[:, 0:4], sb_TK[:, 0:4], sb_acc2[:])
            VF()
            cmul4(sb_acc2[:], sb_P[:, 4:8], sb_TK[:, 8:12])
            V.tensor_add(sb_TK[:, 0:4], sb_TK[:, 0:4], sb_acc2[:])
            VF()
            for nch in range(2):
                cth = sb_cs1[:, nch:nch + 1]
                sth = sb_cs1[:, 2 + nch:3 + nch]
                hre = sb_TK[:, nch:nch + 1]
                him = sb_TK[:, 2 + nch:3 + nch]
                V.tensor_tensor(out=sb_acc[:, 0:1], in0=cth, in1=hre, op=ALU.mult)
                V.tensor_tensor(out=sb_acc[:, 1:2], in0=sth, in1=him, op=ALU.mult)
                VF()
                V.tensor_sub(sb_acc2[:, 0:1], sb_acc[:, 0:1], sb_acc[:, 1:2])
                V.tensor_tensor(out=sb_acc[:, 0:1], in0=sth, in1=hre, op=ALU.mult)
                V.tensor_tensor(out=sb_acc[:, 1:2], in0=cth, in1=him, op=ALU.mult)
                VF()
                V.tensor_add(sb_acc2[:, 1:2], sb_acc[:, 0:1], sb_acc[:, 1:2])
                VF()
                V.tensor_scalar_mul(sb_S0[:, nch:nch + 1], sb_acc2[:, 0:1],
                                    sb_r[:, nch:nch + 1])
                V.tensor_scalar_mul(sb_S0[:, 2 + nch:3 + nch], sb_acc2[:, 1:2],
                                    sb_r[:, nch:nch + 1])
            VF()
            V.tensor_scalar_mul(sb_S0[:, 4:8], sb_S0[:, 0:4], -1.0)
            VF()

            ev_allspill = C.at("gq")

            ev_p = [None] * 8
            ev_rhs = [None]
            ev_yadd = [None] * 8
            ev_h1_last = [None]
            ev_passA_pe = C.at("sp")
            ev_xn_trans = [None]
            ev_p7_prev = [None]

            def load_x_g(tile):
                C.wait(G, ev_x_free[0])
                ev = None
                for tc in range(4):
                    ev = C.dma(G, sb_x[:, tc * D:(tc + 1) * D],
                               x_in[(tile * 4 + tc) * 128:(tile * 4 + tc + 1) * 128, :],
                               "xq")
                return ev

            def reload_y(t, ev_gate):
                C.wait(G, ev_gate)
                if t == 0:
                    C.wait(G, ev_cast_all[0])
                    C.wait(G, ev_cast_all[1])
                    C.wait(G, ev_passA_pe)
                return C.dma(G, sb_y[:], sY[t], "rly")

            def reload_xg(t, ev_gate):
                C.wait(G, ev_gate)
                C.dma(G, sb_xnf[:], sXN[t], "rlx")
                return C.dma(G, sb_gA[:], sG[t], "rlx")

            def rhs_stage(t):
                C.wait(V, ev_h_cons[0])
                if t == 0:
                    C.wait(V, ev_tabB)
                else:
                    V.tensor_mul(sb_acc[:, 0:2], sb_P512[:, 0:2], sb_S0[:, 0:2])
                    V.tensor_mul(sb_acc[:, 2:4], sb_P512[:, 2:4], sb_S0[:, 2:4])
                    V.tensor_mul(sb_acc2[:, 0:2], sb_P512[:, 0:2], sb_S0[:, 2:4])
                    V.tensor_mul(sb_acc2[:, 2:4], sb_P512[:, 2:4], sb_S0[:, 0:2])
                    VF()
                    V.tensor_sub(sb_S0[:, 0:2], sb_acc[:, 0:2], sb_acc[:, 2:4])
                    V.tensor_add(sb_S0[:, 2:4], sb_acc2[:, 0:2], sb_acc2[:, 2:4])
                    VF()
                    V.tensor_scalar_mul(sb_S0[:, 4:8], sb_S0[:, 0:4], -1.0)
                    VF()
                for nch in range(2):
                    cs = sb_cosA[:, nch * T:(nch + 1) * T]
                    sn = sb_sinA[:, nch * T:(nch + 1) * T]
                    rp = sb_rpw[:, nch * T:(nch + 1) * T]
                    V.tensor_scalar_mul(sb_tA[:], cs, sb_S0[:, nch:nch + 1])
                    V.tensor_scalar_mul(sb_tB[:], sn, sb_S0[:, 2 + nch:3 + nch])
                    V.tensor_sub(sb_tA[:], sb_tA[:], sb_tB[:])
                    V.tensor_tensor(out=sb_hre[:, nch * T:(nch + 1) * T],
                                    in0=sb_tA[:], in1=rp, op=ALU.mult)
                    V.tensor_scalar_mul(sb_tA[:], sn, sb_S0[:, 4 + nch:5 + nch])
                    V.tensor_scalar_mul(sb_tB[:], cs, sb_S0[:, 6 + nch:7 + nch])
                    V.tensor_add(sb_tA[:], sb_tA[:], sb_tB[:])
                    V.tensor_tensor(out=sb_himn[:, nch * T:(nch + 1) * T],
                                    in0=sb_tA[:], in1=rp, op=ALU.mult)
                ev_rhs[0] = C.mark_last(V, "sv")

            def corr_stage(t, ev_rly):
                for m in range(8):
                    b = m % 4
                    C.wait(P_, bank_ev[b])
                    if m == 0:
                        C.wait(P_, ev_rhs[0])
                    ins = None
                    first = True
                    for nch in range(2):
                        P_.matmul(pb[b][:],
                                  sb_CreT[:, nch * D + m * 128:nch * D + (m + 1) * 128],
                                  sb_hre[:, nch * T:(nch + 1) * T],
                                  start=first, stop=False)
                        first = False
                        ins = P_.matmul(
                            pb[b][:],
                            sb_CimnT[:, nch * D + m * 128:nch * D + (m + 1) * 128],
                            sb_himn[:, nch * T:(nch + 1) * T],
                            start=False, stop=(nch == 1))
                    evm = C.mark(ins, "sp")
                    if m == 7:
                        ev_h_cons[0] = evm
                    C.wait(V, evm)
                    if m == 0:
                        C.wait(V, ev_rly)
                    ins = V.tensor_tensor(out=sb_y[:, m * T:(m + 1) * T],
                                          in0=pb[b][:],
                                          in1=sb_y[:, m * T:(m + 1) * T],
                                          op=ALU.add)
                    ev_yadd[m] = C.mark(ins, "sv")
                    bank_ev[b] = ev_yadd[m]

            def w2_load(m):
                slot = w2_ring.tick()
                C.wait(G, w2_ring.free_ev[slot])
                ev = C.dma(G, sb_w2r[:, slot * DH1:(slot + 1) * DH1],
                           sW2[m, :, :], w2_ring.sems[slot])
                return (slot, ev)

            ev_rly = reload_y(0, ev_allspill)
            ev_rlx = reload_xg(0, None)
            ev_xl = load_x_g(0)
            rhs_stage(0)
            corr_stage(0, ev_rly)

            for tile in range(NT):
                ev_h1 = [None] * 32
                for ng in range(8):
                    bks = [(ng * 4 + nb) % 6 for nb in range(4)]
                    ins = None
                    for kk in range(0, 8, 2):
                        slots = []
                        for k in (kk, kk + 1):
                            slot = w1_ring.tick()
                            C.wait(Q, w1_ring.free_ev[slot])
                            if tile == 0 and ng == 0 and k == 0:
                                C.wait(Q, ev_cast_all[0])
                                C.wait(Q, ev_cast_all[1])
                            slots.append((k, slot,
                                          C.dma(Q, sb_w1r[:, slot * 512:(slot + 1) * 512],
                                                sW1[k * 128:(k + 1) * 128, ng * 512:(ng + 1) * 512],
                                                w1_ring.sems[slot])))
                        for k, slot, eld in slots:
                            C.wait(P_, eld)
                        if kk == 0:
                            for b in bks:
                                C.wait(P_, bank_ev[b])
                        for k, slot, eld in slots:
                            if ng == 0:
                                C.wait(P_, ev_yadd[k])
                            for nb in range(4):
                                ins = P_.matmul(
                                    pb[bks[nb]][:],
                                    sb_w1r[:, slot * 512 + nb * 128:slot * 512 + (nb + 1) * 128],
                                    sb_y[:, k * T:(k + 1) * T],
                                    start=(k == 0), stop=(k == 7))
                            w1_ring.free_ev[slot] = C.mark(ins, "sp")
                    evk = w1_ring.free_ev[(w1_ring.i - 1) % w1_ring.n]
                    for nb in range(4):
                        n = ng * 4 + nb
                        C.wait(S_, evk)
                        if n == 0:
                            C.wait(S_, ev_h1_last[0])
                        ins = S_.activation(out=sb_big[:, n * T:(n + 1) * T],
                                            in_=pb[bks[nb]][:],
                                            func=AF.Gelu_apprx_tanh)
                        ev_h1[n] = C.mark(ins, "ss")
                        bank_ev[bks[nb]] = ev_h1[n]
                ev_w1_yread = w1_ring.free_ev[(w1_ring.i - 1) % w1_ring.n]

                pre = [w2_load(0), w2_load(1)]
                if tile < NT - 1:
                    ev_rly = reload_y(tile + 1, ev_w1_yread)
                    rhs_stage(tile + 1)
                    corr_stage(tile + 1, ev_rly)

                for m in range(8):
                    slot, eld = pre[m]
                    b = 4 + (m % 2)
                    C.wait(P_, bank_ev[b])
                    C.wait(P_, eld)
                    ins = None
                    for n in range(32):
                        if n == 0:
                            C.wait(P_, ev_h1[31])
                        ins = P_.matmul(
                            pb[b][:],
                            sb_w2r[:, slot * DH1 + n * 128:slot * DH1 + (n + 1) * 128],
                            sb_big[:, n * T:(n + 1) * T],
                            start=(n == 0), stop=(n == 31))
                    w2_ring.free_ev[slot] = C.mark(ins, "sp")
                    if m == 7:
                        ev_h1_last[0] = w2_ring.free_ev[slot]
                    if m + 2 < 8:
                        pre.append(w2_load(m + 2))
                    C.wait(V, w2_ring.free_ev[slot])
                    if m == 0:
                        C.wait(V, ev_rlx)
                        C.wait(V, ev_xn_trans[0])
                    V.scalar_tensor_tensor(out=sb_tA[:],
                                           in0=pb[b][:], scalar=sb_sig[:, 0:1],
                                           in1=sb_xnf[:, m * T:(m + 1) * T],
                                           op0=ALU.mult, op1=ALU.add)
                    ins = V.tensor_tensor(out=sb_xn[:, m * T:(m + 1) * T],
                                          in0=sb_tA[:],
                                          in1=sb_gA[:, m * T:(m + 1) * T],
                                          op=ALU.mult)
                    ev_p[m] = C.mark(ins, "sv")
                    bank_ev[b] = ev_p[m]

                if tile < NT - 1:
                    ev_rlx = reload_xg(tile + 1, ev_p[7])

                ev_odma = None
                insB = None
                for tc in range(4):
                    C.wait(P_, pto_ev[0])
                    C.wait(P_, pto_ev[1])
                    for m in range(8):
                        bk = m // 4
                        C.wait(P_, ev_p[m])
                        ins = P_.transpose(pto[bk][:, (m % 4) * 128:(m % 4 + 1) * 128],
                                           sb_xn[:, m * T + tc * 128:m * T + (tc + 1) * 128],
                                           sb_identb[:])
                        if m == 3:
                            insA = ins
                        if m == 7:
                            insB = ins
                    C.mark(insA, "sp")
                    evB = C.mark(insB, "sp")
                    C.wait(V, evB)
                    if tc == 0:
                        C.wait(V, ev_xl)
                    V.tensor_tensor(out=sb_x[:, tc * D:tc * D + 512],
                                    in0=sb_x[:, tc * D:tc * D + 512],
                                    in1=pto[0][:], op=ALU.add)
                    ins = V.tensor_tensor(out=sb_x[:, tc * D + 512:(tc + 1) * D],
                                          in0=sb_x[:, tc * D + 512:(tc + 1) * D],
                                          in1=pto[1][:], op=ALU.add)
                    ev_add = C.mark(ins, "sv")
                    pto_ev[0] = ev_add
                    pto_ev[1] = ev_add
                    C.wait(G, ev_add)
                    ev_odma = C.dma(G, out_o[(tile * 4 + tc) * 128:(tile * 4 + tc + 1) * 128, :],
                                    sb_x[:, tc * D:(tc + 1) * D], "gq")

                ev_x_free[0] = ev_odma
                ev_xn_trans[0] = C.at("sp")
                if tile < NT - 1:
                    ev_xl = load_x_g(tile + 1)

            G.wait_ge(sems["gq"], C.cnt["gq"])

        @block.sync
        def _(sync):
            emit_program(Ctx(nc, "snc", sems))

        @block.gpsimd
        def _(gpsimd):
            emit_program(Ctx(nc, "gps", sems))

        @block.vector
        def _(vector):
            emit_program(Ctx(nc, "vec", sems))

        @block.scalar
        def _(scalar):
            emit_program(Ctx(nc, "scl", sems))

        @block.tensor
        def _(tensor):
            emit_program(Ctx(nc, "pe", sems))

    return nc


_NC_CACHE = [None]


def kernel(L0_attn_x3d, L0_attn_nu_log, L0_attn_theta_log, L0_attn_B_re,
           L0_attn_B_im, L0_attn_C_re, L0_attn_C_im, L0_attn_D_mat, L0_attn_W1,
           L0_attn_W2, L0_attn_res_logit, L0_ffn_w_gate, L0_ffn_w_up,
           L0_ffn_w_down):
    x = np.ascontiguousarray(np.asarray(L0_attn_x3d, np.float32).reshape(2 * 16384, D))
    BreT = np.ascontiguousarray(np.asarray(L0_attn_B_re, np.float32).T)
    BimT = np.ascontiguousarray(np.asarray(L0_attn_B_im, np.float32).T)
    CreT = np.ascontiguousarray(np.asarray(L0_attn_C_re, np.float32).T)
    CimT = np.ascontiguousarray(np.asarray(L0_attn_C_im, np.float32).T)
    DT = np.ascontiguousarray(np.asarray(L0_attn_D_mat, np.float32).T)
    W1 = np.ascontiguousarray(np.asarray(L0_attn_W1, np.float32))
    W2 = np.ascontiguousarray(np.asarray(L0_attn_W2, np.float32))
    WGT = np.ascontiguousarray(np.asarray(L0_ffn_w_gate, np.int32).T)
    WUT = np.ascontiguousarray(np.asarray(L0_ffn_w_up, np.int32).T)
    WDT = np.ascontiguousarray(np.asarray(L0_ffn_w_down, np.int32).T)
    nu = np.ascontiguousarray(np.asarray(L0_attn_nu_log, np.float32).reshape(NST, 1))
    th = np.ascontiguousarray(np.asarray(L0_attn_theta_log, np.float32).reshape(NST, 1))
    rsl = np.ascontiguousarray(
        np.broadcast_to(np.asarray(L0_attn_res_logit, np.float32).reshape(1, 1),
                        (128, 1)))

    if _NC_CACHE[0] is None:
        _NC_CACHE[0] = build_nc()
    nc = _NC_CACHE[0]

    in_maps = []
    for c in range(NCORE):
        msel = np.zeros((8, 3), np.float32)
        for k in range(3):
            j = c - 1 - k
            if j >= 0 and (j // 4) == (c // 4):
                msel[j, k] = 1.0
        msel = np.ascontiguousarray(
            np.broadcast_to(msel.reshape(1, 24), (128, 24)).astype(np.float32))
        in_maps.append(dict(
            x=np.ascontiguousarray(x[c * SEG:(c + 1) * SEG]),
            BreT=BreT, BimT=BimT, CreT=CreT, CimT=CimT, DT=DT,
            W1=W1, W2=W2, WGT=WGT, WUT=WUT, WDT=WDT,
            nu_log=nu, th_log=th, rsl=rsl, msel=msel))
    res = run_bass_kernel_spmd(nc, in_maps, core_ids=list(range(NCORE)))
    out = np.concatenate([np.asarray(r["out"], np.float32) for r in res.results],
                         axis=0)
    return out.reshape(2, 16384, D)
